# version 08
# baseline (speedup 1.0000x reference)
import math
import os
from contextlib import ExitStack

import numpy as np
import ml_dtypes

import concourse.bass as bass
import concourse.mybir as mybir
from concourse.bass_utils import run_bass_kernel_spmd

F32 = mybir.dt.float32
BF16 = mybir.dt.bfloat16
FP8 = mybir.dt.float8e4
U8 = mybir.dt.uint8
I32 = mybir.dt.int32
ALU = mybir.AluOpType
AFT = mybir.ActivationFunctionType

HID = 2048
SEQ = 2048
HQ, HKV, D = 16, 8, 128
I_FF = 6144
NCORES = 8
NSEQ = 2
NMACRO = 4
TM = 512
NTB = TM // 128
NBLK = TM // 16
KC = HID // 128
OC_FF = I_FF // 128

PI = math.pi
C1 = float(np.float32(2 * PI))
C2 = float(np.float32(np.float64(2 * PI) - np.float64(np.float32(2 * PI))))
INV2PI = float(np.float32(1.0 / (2 * PI)))
LN_EPS = 1e-5
MAGIC = 12582912.0
SAFE = set(os.environ.get('KSAFE', 'down').split(','))


class Sem:
    def __init__(self, nc, es, name):
        self.h = es.enter_context(nc.semaphore(name))
        self.n = 0


class Emitter:
    def __init__(self):
        self.prog = {e: [] for e in ("sync", "vector", "scalar", "tensor")}

    def op(self, eng, fn, inc=None):
        if inc is not None:
            sem, k = inc
            def run(e, fn=fn, sem=sem, k=k):
                fn(e).then_inc(sem.h, k)
            sem.n += k
        else:
            run = fn
        self.prog[eng].append(run)

    def wait(self, eng, sem, val=None):
        v = sem.n if val is None else val
        if v > 0:
            self.prog[eng].append(lambda e, s=sem, v=v: e.wait_ge(s.h, v))

    def barrier(self, bar, self_sems=None):
        engines = ("sync", "vector", "scalar", "tensor")
        if self_sems:
            for e in engines:
                for s in self_sems.get(e, ()):
                    self.wait(e, s)
        for e in engines:
            self.op(e, lambda x: x.nop(), inc=(bar[e], 1))
        for e in engines:
            for o in engines:
                if o != e:
                    self.wait(e, bar[o])


def _rope_tables():
    import jax
    cpu = jax.devices("cpu")[0]
    import jax.numpy as jnp
    with jax.default_device(cpu):
        inv_freq = 1.0 / 10000.0 ** (jnp.arange(0, D, 2, dtype=jnp.float32) / D)
        freqs = jnp.arange(SEQ, dtype=jnp.float32)[:, None] * inv_freq[None, :]
        emb = jnp.concatenate((freqs, freqs), axis=-1)
        cos = np.asarray(jnp.cos(emb))
        sin = np.asarray(jnp.sin(emb))
    cosT = np.ascontiguousarray(cos.T).astype(np.float32)
    sinT = np.ascontiguousarray(sin.T).astype(np.float32)
    return cosT, sinT


def _consts():
    cosT, sinT = _rope_tables()
    rotT = np.zeros((128, 128), np.float32)
    for d in range(64):
        rotT[d + 64, d] = -1.0
        rotT[d, d + 64] = 1.0
    maskM = np.zeros((128, 256), np.float32)
    for tau in range(16):
        for j in range(8):
            for h in range(16):
                maskM[tau * 8 + j, tau * 16 + h] = 1.0 / 64.0
    return {
        "cosT": cosT,
        "sinSS": sinT,
        "rotT": rotT.astype(ml_dtypes.bfloat16),
        "maskM": maskM.astype(ml_dtypes.bfloat16),
        "identb": np.eye(128, dtype=ml_dtypes.bfloat16),
        "identf": np.eye(128, dtype=np.float32),
    }


def build_nc(nseq=NSEQ, nmacro=NMACRO, dbg=False):
    nc = bass.Bass("TRN2", target_bir_lowering=False, detect_race_conditions=False)
    es = ExitStack()

    din = {}
    for name, shp, dt in [
        ("x", [nseq, SEQ, HID], F32),
        ("wq", [HID, HID], F32), ("bq", [HID], F32),
        ("wk", [HKV * D, HID], F32), ("bk", [HKV * D], F32),
        ("wv", [HKV * D, HID], F32), ("bv", [HKV * D], F32),
        ("wo", [HID, HID], F32), ("bo", [HID], F32),
        ("rmat", [64, 128], F32),
        ("gw", [HID, I_FF], F32), ("uw", [HID, I_FF], F32), ("dw", [I_FF, HID], F32),
        ("gs", [I_FF], F32), ("us", [I_FF], F32), ("ds", [HID], F32),
        ("cosT", [128, SEQ], F32), ("sinSS", [128, SEQ], F32),
        ("rotT", [128, 128], BF16), ("maskM", [128, 256], BF16),
        ("identb", [128, 128], BF16), ("identf", [128, 128], F32),
    ]:
        din[name] = nc.dram_tensor(name, shp, dt, kind="ExternalInput").ap()
    y_out = nc.dram_tensor("y", [nseq, SEQ, HID], F32, kind="ExternalOutput").ap()
    dbg_t = {}
    if dbg:
        for nm, shp, dt in [("dbg_am", [128, 16 * SEQ], BF16),
                            ("dbg_sfm", [128, 16 * TM], BF16),
                            ("dbg_fuse", [128, OC_FF * TM], BF16),
                            ("dbg_xn", [KC, 128, SEQ], BF16),
                            ("dbg_wqT", [KC, 128, 128], BF16),
                            ("dbg_qps", [64, 16 * TM], BF16),
                            ("dbg_kps", [64, 8 * TM], BF16),
                            ("dbg_vall", [128, 8 * TM], BF16),
                            ("dbg_pred", [64, 1024], F32),
                            ("dbg_psb", [64, 1024], F32),
                            ("dbg_rbf", [64, 1024], F32),
                            ("dbg_qrot", [128, TM], BF16),
                            ("dbg_sqf", [128, TM], F32),
                            ("dbg_nbf", [64, 1024], F32),
                            ("dbg_sfm3", [128, 16 * TM], BF16),
                            ("dbg_fuse3", [128, OC_FF * TM], BF16),
                            ("dbg_g3", [128, TM], F32)]:
            dbg_t[nm] = nc.dram_tensor(nm, shp, dt, kind="ExternalOutput").ap()

    scr = {
        "wq_nat": nc.dram_tensor("wq_nat", [HID, HID], BF16, kind="Internal").ap(),
        "wk_nat": nc.dram_tensor("wk_nat", [HKV * D, HID], BF16, kind="Internal").ap(),
        "wv_nat": nc.dram_tensor("wv_nat", [HKV * D, HID], BF16, kind="Internal").ap(),
        "wo_nat": nc.dram_tensor("wo_nat", [HID, HID], BF16, kind="Internal").ap(),
        "wqT": nc.dram_tensor("wqT", [16, KC, 128, 128], BF16, kind="Internal").ap(),
        "wkT": nc.dram_tensor("wkT", [8, KC, 128, 128], BF16, kind="Internal").ap(),
        "wvT": nc.dram_tensor("wvT", [8, KC, 128, 128], BF16, kind="Internal").ap(),
        "woT": nc.dram_tensor("woT", [16, KC, 128, 128], BF16, kind="Internal").ap(),
        "g8": nc.dram_tensor("g8", [HID, I_FF], U8, kind="Internal").ap(),
        "u8": nc.dram_tensor("u8", [HID, I_FF], U8, kind="Internal").ap(),
        "d8": nc.dram_tensor("d8", [8, OC_FF, 128, 256], U8, kind="Internal").ap(),
        "xn_s": nc.dram_tensor("xn_s", [nseq, KC, 128, SEQ], BF16, kind="Internal").ap(),
    }

    em = Emitter()
    O, W = em.op, em.wait

    es.enter_context(nc.allow_low_precision(
        "reference quantizes FFN weights to fp8-e4m3; bf16 matmul operands with "
        "fp32 accumulation are within this problem's stated tolerance"))
    with es:
        sb = lambda name, shp, dt: es.enter_context(nc.sbuf_tensor(name, shp, dt))
        S = lambda name: Sem(nc, es, name)

        cos_t = sb("cos_t", [128, SEQ], F32)
        sinss_t = sb("sinss_t", [128, SEQ], F32)
        rot_t = sb("rot_t", [128, 128], BF16)
        mask_t = sb("mask_t", [128, 256], BF16)
        identb_t = sb("identb_t", [128, 128], BF16)
        identf_t = sb("identf_t", [128, 128], F32)
        ones64 = sb("ones64", [128, 64], F32)
        rT_t = sb("rT_t", [128, 64], BF16)
        bq_t = sb("bq_t", [128, 16], F32)
        bk_t = sb("bk_t", [128, 8], F32)
        bv_t = sb("bv_t", [128, 8], F32)
        bo_t = sb("bo_t", [128, 16], F32)
        gs_t = sb("gs_t", [128, 48], F32)
        us_t = sb("us_t", [128, 48], F32)
        ds_t = sb("ds_t", [128, 16], F32)
        am_sb = sb("am_sb", [128, 16 * SEQ], BF16)
        xn_fm = sb("xn_fm", [128, KC * TM], BF16)
        wslab = [sb(f"wslab{i}", [128, 2048], BF16) for i in range(2)]

        pf = [es.enter_context(nc.psum_tensor(f"pf{i}", [128, 512], F32))
              for i in range(8)]
        pbf = pf[3][:].bitcast(BF16)

        s_ld = S("s_ld")
        s_x = S("s_x")
        s_w = S("s_w")
        s_sp = S("s_sp")
        s_te = S("s_te")
        s_ve = S("s_ve")
        s_se = S("s_se")
        s_st = S("s_st")
        bar = {e: S("bar_" + e) for e in ("sync", "vector", "scalar", "tensor")}
        self_sems = {"sync": (s_ld, s_x, s_w, s_sp, s_st), "vector": (s_ve,),
                     "scalar": (s_se,), "tensor": (s_te,)}

        for src, dst in [(din["cosT"], cos_t), (din["sinSS"], sinss_t),
                         (din["rotT"], rot_t), (din["maskM"], mask_t),
                         (din["identb"], identb_t), (din["identf"], identf_t)]:
            O("sync", lambda e, s=src, d=dst: e.dma_start(out=d[:], in_=s), inc=(s_ld, 16))
        O("vector", lambda e: e.memset(ones64[:], 1.0))
        O("vector", lambda e: e.memset(am_sb[:], 0.0), inc=(s_ve, 1))

        btmp = sb("btmp", [64, 128], F32)
        bias_specs = [(din["bq"], 16, bq_t), (din["bk"], 8, bk_t), (din["bv"], 8, bv_t),
                      (din["bo"], 16, bo_t), (din["gs"], 48, gs_t), (din["us"], 48, us_t),
                      (din["ds"], 16, ds_t)]
        W("sync", s_ld)
        for bsrc, n, bdst in bias_specs:
            W("sync", s_te)
            src_ap = bsrc.rearrange("(a b) -> a b", b=128)
            O("sync", lambda e, s=src_ap, n=n: e.dma_start(out=btmp[0:n, :], in_=s),
              inc=(s_ld, 16))
            W("tensor", s_ld)
            W("tensor", s_se)
            O("tensor", lambda e, n=n: e.transpose(pf[0][:, 0:n], btmp[0:n, :],
                                                   identf_t[0:n, 0:n]), inc=(s_te, 1))
            W("scalar", s_te)
            O("scalar", lambda e, n=n, d=bdst: e.copy(out=d[:, 0:n], in_=pf[0][:, 0:n]),
              inc=(s_se, 1))
        W("sync", s_te)
        O("sync", lambda e: e.dma_start(out=btmp[0:64, :], in_=din["rmat"]), inc=(s_ld, 16))
        W("tensor", s_ld)
        W("tensor", s_se)
        O("tensor", lambda e: e.transpose(pf[0][:, 0:64], btmp[0:64, :],
                                          identf_t[0:64, 0:64]), inc=(s_te, 1))
        W("scalar", s_te)
        O("scalar", lambda e: e.copy(out=rT_t[:], in_=pf[0][:, 0:64]), inc=(s_se, 1))

        with nc.sbuf_tensor("w32", [128, I_FF], F32) as w32, \
             nc.sbuf_tensor("wbf", [128, I_FF // 2], BF16) as wbf, \
             nc.sbuf_tensor("w8", [128, I_FF], U8) as w8:
            for wsrc, wnat, nrow in [(din["wq"], scr["wq_nat"], 16),
                                     (din["wk"], scr["wk_nat"], 8),
                                     (din["wv"], scr["wv_nat"], 8),
                                     (din["wo"], scr["wo_nat"], 16)]:
                for r in range(nrow):
                    W("sync", s_sp)
                    O("sync", lambda e, s=wsrc, r=r: e.dma_start(
                        out=w32[:, 0:HID], in_=s[r * 128:(r + 1) * 128, :]),
                        inc=(s_ld, 16))
                    W("vector", s_ld)
                    O("vector", lambda e: e.tensor_copy(out=wbf[:, 0:HID],
                                                        in_=w32[:, 0:HID]), inc=(s_ve, 1))
                    W("sync", s_ve)
                    O("sync", lambda e, d=wnat, r=r: e.dma_start(
                        out=d[r * 128:(r + 1) * 128, :], in_=wbf[:, 0:HID]),
                        inc=(s_sp, 16))
            for wnat, wT, noc in [(scr["wq_nat"], scr["wqT"], 16),
                                  (scr["wk_nat"], scr["wkT"], 8),
                                  (scr["wv_nat"], scr["wvT"], 8),
                                  (scr["wo_nat"], scr["woT"], 16)]:
                for kc in range(KC):
                    W("sync", s_sp)
                    O("sync", lambda e, s=wnat, kc=kc, nr=noc * 128: e.dma_start(
                        out=wbf[:, 0:nr], in_=s[:, kc * 128:(kc + 1) * 128],
                        transpose=True), inc=(s_ld, 16))
                    W("sync", s_ld)
                    dst_ap = wT[:, kc].rearrange("oc p o -> p oc o")
                    O("sync", lambda e, d=dst_ap, n=noc * 128: e.dma_start(
                        out=d, in_=wbf[:, 0:n]), inc=(s_sp, 16))
            for wsrc, wdst in [(din["gw"], scr["g8"]), (din["uw"], scr["u8"])]:
                for r in range(KC):
                    W("sync", s_sp)
                    O("sync", lambda e, s=wsrc, r=r: e.dma_start(
                        out=w32[:], in_=s[r * 128:(r + 1) * 128, :]), inc=(s_ld, 16))
                    W("vector", s_ld)
                    O("vector", lambda e: e.tensor_copy(out=w8[:].bitcast(FP8), in_=w32[:]),
                      inc=(s_ve, 1))
                    W("sync", s_ve)
                    O("sync", lambda e, d=wdst, r=r: e.dma_start(
                        out=d[r * 128:(r + 1) * 128, :], in_=w8[:]), inc=(s_sp, 16))
            for r in range(OC_FF):
                W("sync", s_sp)
                O("sync", lambda e, r=r: e.dma_start(
                    out=w32[:, 0:HID], in_=din["dw"][r * 128:(r + 1) * 128, :]),
                    inc=(s_ld, 16))
                W("vector", s_ld)
                O("vector", lambda e: e.tensor_copy(out=w8[:, 0:HID].bitcast(FP8),
                                                    in_=w32[:, 0:HID]), inc=(s_ve, 1))
                W("sync", s_ve)
                dst_ap = scr["d8"][:, r].rearrange("g p o -> p g o")
                O("sync", lambda e, d=dst_ap: e.dma_start(
                    out=d, in_=w8[:, 0:HID]), inc=(s_sp, 16))
        em.barrier(bar, self_sems)

        def phase_C(s, cbuf):
            (qp_sin, qp_cos, kp_sin, kp_cos, v_all, vt_sb, vt_sb2, sm_sb, sm_sb2, xt, scr2k, xn_tm,
             ln, qbf, qrot, t1f, t2f, sqf, nbf, rbf, p_sb, pred) = cbuf
            ln_sx, ln_sq, ln_mu, ln_t0, ln_t1, ln_rstd, ln_nb = ln
            for m in range(nmacro):
                T0 = m * TM
                for tb in range(NTB):
                    W("sync", s_se)
                    O("sync", lambda e, s=s, t=T0 + tb * 128: e.dma_start(
                        out=xt[:], in_=din["x"][s, t:t + 128, :]), inc=(s_x, 16))
                    W("vector", s_x)
                    O("vector", lambda e: e.tensor_scalar(
                        out=scr2k[:], in0=xt[:], scalar1=1.0, scalar2=None,
                        op0=ALU.mult, op1=ALU.add, accum_out=ln_sx[:]), inc=(s_ve, 1))
                    W("scalar", s_x)
                    W("scalar", s_ve)
                    O("scalar", lambda e: e.activation(
                        out=scr2k[:], in_=xt[:], func=AFT.Square,
                        accum_out=ln_sq[:]), inc=(s_se, 1))
                    W("vector", s_se)
                    O("vector", lambda e: e.tensor_scalar(
                        out=ln_mu[:], in0=ln_sx[:], scalar1=1.0 / HID, scalar2=None,
                        op0=ALU.mult))
                    O("vector", lambda e: e.tensor_scalar(
                        out=ln_t1[:], in0=ln_sq[:], scalar1=1.0 / HID, scalar2=None,
                        op0=ALU.mult))
                    O("vector", lambda e: e.scalar_tensor_tensor(
                        out=ln_t0[:], in0=ln_mu[:], scalar=1.0, in1=ln_mu[:],
                        op0=ALU.mult, op1=ALU.mult))
                    O("vector", lambda e: e.scalar_tensor_tensor(
                        out=ln_t1[:], in0=ln_t0[:], scalar=-1.0, in1=ln_t1[:],
                        op0=ALU.mult, op1=ALU.add))
                    O("vector", lambda e: e.tensor_scalar(
                        out=ln_t1[:], in0=ln_t1[:], scalar1=1.0, scalar2=LN_EPS,
                        op0=ALU.mult, op1=ALU.add), inc=(s_ve, 1))
                    W("scalar", s_ve)
                    O("scalar", lambda e: e.activation(out=ln_t1[:], in_=ln_t1[:],
                                                       func=AFT.Ln))
                    O("scalar", lambda e: e.activation(out=ln_rstd[:], in_=ln_t1[:],
                                                       func=AFT.Exp, scale=np.float32(-0.5)),
                      inc=(s_se, 1))
                    W("vector", s_se)
                    O("vector", lambda e: e.scalar_tensor_tensor(
                        out=ln_nb[:], in0=ln_mu[:], scalar=-1.0, in1=ln_rstd[:],
                        op0=ALU.mult, op1=ALU.mult), inc=(s_ve, 1))
                    W("scalar", s_ve)
                    O("scalar", lambda e: e.activation(
                        out=xn_tm[:], in_=xt[:], func=AFT.Identity, scale=ln_rstd[:],
                        bias=ln_nb[:]), inc=(s_se, 1))
                    W("tensor", s_se)
                    for g in range(4):
                        W("tensor", s_se)
                        for i in range(4):
                            kc = g * 4 + i
                            O("tensor", lambda e, kc=kc, i=i: e.transpose(
                                pbf[:, i * 128:(i + 1) * 128],
                                xn_tm[:, kc * 128:(kc + 1) * 128], identb_t[:]),
                                inc=(s_te, 1) if i == 3 else None)
                        W("scalar", s_te)
                        dst = xn_fm[:].rearrange("p (kc t) -> p kc t", kc=KC)[
                            :, g * 4:(g + 1) * 4, tb * 128:(tb + 1) * 128]
                        O("scalar", lambda e, d=dst: e.activation(
                            out=d, in_=pbf[:, 0:512], func=AFT.Copy), inc=(s_se, 1))
                W("sync", s_se)
                dst_ap = scr["xn_s"][s, :, :, T0:T0 + TM].rearrange("kc p t -> p kc t")
                O("sync", lambda e, a=dst_ap: e.dma_start(
                    out=a, in_=xn_fm[:]), inc=(s_sp, 16))

                heads = [("q", h) for h in range(HQ)] + \
                        [("k", h) for h in range(HKV)] + \
                        [("v", h) for h in range(HKV)]
                te_after = {}
                se_after = {}

                def qkv_load(i):
                    proj, h = heads[i]
                    wT = {"q": scr["wqT"], "k": scr["wkT"], "v": scr["wvT"]}[proj]
                    buf = wslab[i % 2]
                    if i - 2 >= 0:
                        W("sync", s_te, te_after[i - 2])
                    src_ap = wT[h].rearrange("kc p o -> p kc o")
                    O("sync", lambda e, a=src_ap, b=buf: e.dma_start(
                        out=b[:, 0:2048], in_=a), inc=(s_w, 16))

                def qkv_chain(i):
                    proj, h = heads[i]
                    buf = wslab[i % 2]
                    pacc = pf[i % 2]
                    W("tensor", s_w)
                    if i - 2 >= 0:
                        W("tensor", s_se, se_after[i - 2])
                    else:
                        W("tensor", s_se)
                    for kc in range(KC):
                        O("tensor", lambda e, b=buf, kc=kc, p=pacc: e.matmul(
                            p[:], b[:, kc * 128:(kc + 1) * 128],
                            xn_fm[:, kc * TM:(kc + 1) * TM],
                            start=(kc == 0), stop=(kc == KC - 1)),
                            inc=(s_te, 1) if kc == KC - 1 else None)
                    te_after[i] = s_te.n

                def qkv_evac(i):
                    proj, h = heads[i]
                    pacc = pf[i % 2]
                    W("scalar", s_te, te_after[i])
                    if proj == "v":
                        dst = v_all[:].rearrange("p (t j) -> p t j", j=8)[:, :, h]
                        O("scalar", lambda e, d=dst, p=pacc, h=h: e.activation(
                            out=d, in_=p[:], func=AFT.Identity, bias=bv_t[:, h:h + 1]),
                            inc=(s_se, 1))
                    else:
                        bias = {"q": bq_t, "k": bk_t}[proj]
                        W("scalar", s_ve)
                        O("scalar", lambda e, p=pacc, h=h, b=bias: e.activation(
                            out=qbf[:], in_=p[:], func=AFT.Identity, bias=b[:, h:h + 1]),
                            inc=(s_se, 1))
                    se_after[i] = s_se.n

                def qkv_epi(i):
                    proj, h = heads[i]
                    if proj == "v":
                        return
                    W("tensor", s_se, se_after[i])
                    W("tensor", s_ve)
                    O("tensor", lambda e: e.matmul(pf[2][:], rot_t[:], qbf[:],
                                                   start=True, stop=True), inc=(s_te, 1))
                    W("vector", s_te)
                    O("vector", lambda e, T0=T0: e.tensor_tensor(
                        out=t1f[:], in0=qbf[:], in1=cos_t[:, T0:T0 + TM], op=ALU.mult))
                    O("vector", lambda e, T0=T0: e.tensor_tensor(
                        out=t2f[:], in0=pf[2][:], in1=sinss_t[:, T0:T0 + TM],
                        op=ALU.mult))
                    O("vector", lambda e: e.tensor_tensor(
                        out=qrot[:], in0=t1f[:], in1=t2f[:], op=ALU.add), inc=(s_ve, 1))
                    W("scalar", s_ve)
                    O("scalar", lambda e: e.activation(
                        out=sqf[:], in_=qrot[:], func=AFT.Square), inc=(s_se, 1))
                    W("tensor", s_se)
                    W("tensor", s_ve)
                    pn = pf[6 + (h % 2)]
                    O("tensor", lambda e, pn=pn: e.matmul(
                        pn[0:64, :], ones64[:, 0:64], sqf[:],
                        start=True, stop=True))
                    pq = pf[4 + (h % 2)]
                    O("tensor", lambda e, pq=pq: e.matmul(
                        pq[0:64, :], rT_t[:], qrot[:], start=True, stop=True),
                        inc=(s_te, 1))
                    if h % 2 == 1:
                        featmap_pack(proj, h - 1, 16 if proj == "q" else 8)

                def featmap_pack(proj, hA, stride):
                    sink = qp_sin if proj == "q" else kp_sin
                    cosk = qp_cos if proj == "q" else kp_cos
                    W("vector", s_te)
                    O("vector", lambda e: e.tensor_scalar(
                        out=nbf[:, 0:512], in0=pf[6][0:64, :], scalar1=1e-12,
                        scalar2=None, op0=ALU.max))
                    O("vector", lambda e: e.tensor_scalar(
                        out=nbf[:, 512:1024], in0=pf[7][0:64, :], scalar1=1e-12,
                        scalar2=None, op0=ALU.max), inc=(s_ve, 1))
                    W("scalar", s_ve)
                    O("scalar", lambda e: e.activation(out=nbf[:], in_=nbf[:],
                                                       func=AFT.Ln))
                    O("scalar", lambda e: e.activation(out=rbf[:], in_=nbf[:],
                                                       func=AFT.Exp, scale=np.float32(-0.5)),
                      inc=(s_se, 1))
                    W("vector", s_se)
                    O("vector", lambda e: e.tensor_tensor(
                        out=p_sb[:, 0:512], in0=pf[4][0:64, :], in1=rbf[:, 0:512],
                        op=ALU.mult))
                    O("vector", lambda e: e.tensor_tensor(
                        out=p_sb[:, 512:1024], in0=pf[5][0:64, :], in1=rbf[:, 512:1024],
                        op=ALU.mult))
                    kft = nbf
                    O("vector", lambda e: e.tensor_scalar(
                        out=ki_sb[:].bitcast(F32), in0=p_sb[:], scalar1=INV2PI,
                        scalar2=MAGIC, op0=ALU.mult, op1=ALU.add))
                    O("vector", lambda e, kft=kft: e.tensor_scalar(
                        out=kft[:], in0=ki_sb[:].bitcast(F32), scalar1=1.0,
                        scalar2=-MAGIC, op0=ALU.mult, op1=ALU.add))
                    O("vector", lambda e, kft=kft: e.scalar_tensor_tensor(
                        out=pred[:], in0=kft[:], scalar=-C1, in1=p_sb[:],
                        op0=ALU.mult, op1=ALU.add))
                    O("vector", lambda e: e.tensor_scalar(
                        out=pred[:], in0=pred[:], scalar1=PI, scalar2=-PI,
                        op0=ALU.min, op1=ALU.max), inc=(s_ve, 1))
                    W("scalar", s_ve)
                    sdst = sink[:].rearrange("p (t s) -> p t s", s=stride)[
                        :, :, hA:hA + 2].transpose([0, 2, 1])
                    cdst = cosk[:].rearrange("p (t s) -> p t s", s=stride)[
                        :, :, hA:hA + 2].transpose([0, 2, 1])
                    psrc = pred[:].rearrange("p (a t) -> p a t", a=2)
                    O("scalar", lambda e, d=sdst, s2=psrc: e.activation(
                        out=d, in_=s2, func=AFT.Sin))
                    O("scalar", lambda e: e.activation(out=p_sb[:], in_=pred[:],
                                                       func=AFT.Abs))
                    pasrc = p_sb[:].rearrange("p (a t) -> p a t", a=2)
                    O("scalar", lambda e, d=cdst, s2=pasrc: e.activation(
                        out=d, in_=s2, func=AFT.Sin, scale=np.float32(-1.0), bias=np.float32(PI / 2)),
                        inc=(s_se, 1))

                qkv_load(0)
                qkv_chain(0)
                for i in range(len(heads)):
                    if "qkv" in SAFE:
                        em.barrier(bar, self_sems)
                    if i + 1 < len(heads):
                        qkv_load(i + 1)
                    qkv_evac(i)
                    if i + 1 < len(heads):
                        qkv_chain(i + 1)
                    qkv_epi(i)

                W("tensor", s_se)
                W("tensor", s_ve)
                bl_mask_ve = {}
                bl_vt_se = {}
                bl_am_se = {}
                for b in range(NBLK):
                    if "bil" in SAFE:
                        em.barrier(bar, self_sems)
                    Babs = m * NBLK + b
                    par = b % 2
                    pS = pf[2] if par == 0 else pf[6]
                    pO = pf[4] if par == 0 else pf[5]
                    pV = pbf if par == 0 else pf[7][:].bitcast(BF16)
                    vt = vt_sb if par == 0 else vt_sb2
                    sm = sm_sb if par == 0 else sm_sb2
                    if b - 2 >= 0:
                        W("tensor", s_ve, bl_mask_ve[b - 2])
                    O("tensor", lambda e, b=b, pS=pS: e.matmul(
                        pS[:, 0:256], kp_sin[:, b * 128:(b + 1) * 128],
                        qp_sin[:, b * 256:(b + 1) * 256], start=True, stop=False))
                    O("tensor", lambda e, b=b, pS=pS: e.matmul(
                        pS[:, 0:256], kp_cos[:, b * 128:(b + 1) * 128],
                        qp_cos[:, b * 256:(b + 1) * 256], start=False, stop=True),
                        inc=(s_te, 1))
                    sK = s_te.n
                    if b - 2 >= 0:
                        W("tensor", s_se, bl_vt_se[b - 2])
                    O("tensor", lambda e, b=b, pV=pV: e.transpose(
                        pV[:, 0:128], v_all[:, b * 128:(b + 1) * 128], identb_t[:]),
                        inc=(s_te, 1))
                    tV = s_te.n
                    W("vector", s_te, sK)
                    O("vector", lambda e, pS=pS, sm=sm: e.tensor_tensor(
                        out=sm[:], in0=pS[:, 0:256], in1=mask_t[:], op=ALU.mult),
                        inc=(s_ve, 1))
                    bl_mask_ve[b] = s_ve.n
                    W("scalar", s_te, tV)
                    O("scalar", lambda e, pV=pV, vt=vt: e.copy(out=vt[:], in_=pV[:, 0:128]),
                      inc=(s_se, 1))
                    bl_vt_se[b] = s_se.n
                    W("tensor", s_ve, bl_mask_ve[b])
                    W("tensor", s_se, bl_vt_se[b])
                    if b - 2 >= 0:
                        W("tensor", s_se, bl_am_se[b - 2])
                    O("tensor", lambda e, vt=vt, sm=sm, pO=pO: e.matmul(
                        pO[:, 0:256], vt[:], sm[:], start=True, stop=True), inc=(s_te, 1))
                    tO = s_te.n
                    W("scalar", s_te, tO)
                    adst = am_sb[:].rearrange("p (jj h t) -> p jj h t", jj=16, h=16)[
                        :, :, :, Babs]
                    asrc = pO[:, 0:256].rearrange("p (jj h) -> p jj h", jj=16)
                    O("scalar", lambda e, d=adst, s2=asrc: e.activation(
                        out=d, in_=s2, func=AFT.Copy), inc=(s_se, 1))
                    bl_am_se[b] = s_se.n

        def phase_W(s, m, s_fm):
            w_te = {}
            w_se = {}

            def wload(oc):
                buf = wslab[oc % 2]
                if oc - 2 >= 0:
                    W("sync", s_te, w_te[oc - 2])
                src_ap = scr["woT"][oc].rearrange("kc p o -> p kc o")
                O("sync", lambda e, a=src_ap, b=buf: e.dma_start(
                    out=b[:, 0:2048], in_=a), inc=(s_w, 16))

            def wchain(oc):
                buf = wslab[oc % 2]
                pacc = pf[oc % 2]
                W("tensor", s_w)
                if oc - 2 >= 0:
                    W("tensor", s_se, w_se[oc - 2])
                for jj in range(16):
                    O("tensor", lambda e, jj=jj, b=buf, p=pacc: e.matmul(
                        p[:], b[:, jj * 128:(jj + 1) * 128],
                        am_sb[:, jj * SEQ + m * TM: jj * SEQ + (m + 1) * TM],
                        start=(jj == 0), stop=(jj == 15)),
                        inc=(s_te, 1) if jj == 15 else None)
                w_te[oc] = s_te.n

            def wevac(oc):
                pacc = pf[oc % 2]
                W("scalar", s_te, w_te[oc])
                O("scalar", lambda e, oc=oc, p=pacc: e.activation(
                    out=s_fm[:, oc * TM:(oc + 1) * TM], in_=p[:], func=AFT.Identity,
                    bias=bo_t[:, oc:oc + 1]), inc=(s_se, 1))
                w_se[oc] = s_se.n

            wload(0)
            wchain(0)
            for oc in range(16):
                if "w" in SAFE:
                    em.barrier(bar, self_sems)
                if oc + 1 < 16:
                    wload(oc + 1)
                wevac(oc)
                if oc + 1 < 16:
                    wchain(oc + 1)

        def phase_B(s, m, bbuf):
            (s_fm, fuse, gu8, uu8, d8s, gtmp, uptmp, gatmp, sgtmp, xh) = bbuf
            T0 = m * TM
            W("sync", s_te)
            src_ap = scr["xn_s"][s, :, :, T0:T0 + TM].rearrange("kc p t -> p kc t")
            O("sync", lambda e, a=src_ap: e.dma_start(
                out=xn_fm[:], in_=a), inc=(s_x, 16))
            W("tensor", s_x)
            gu_te = {}
            gu_se = {}

            def guload(og):
                buf_g, buf_u = gu8[og % 2], uu8[og % 2]
                if og - 2 >= 0:
                    W("sync", s_te, gu_te[og - 2])
                gsrc = scr["g8"][:, og * 256:(og + 1) * 256].rearrange(
                    "(kc p) o -> p kc o", p=128)
                usrc = scr["u8"][:, og * 256:(og + 1) * 256].rearrange(
                    "(kc p) o -> p kc o", p=128)
                O("sync", lambda e, a=gsrc, d=buf_g: e.dma_start(out=d[:, 0:4096], in_=a),
                  inc=(s_w, 16))
                O("sync", lambda e, a=usrc, d=buf_u: e.dma_start(out=d[:, 0:4096], in_=a),
                  inc=(s_w, 16))

            def guchain(og):
                buf_g, buf_u = gu8[og % 2], uu8[og % 2]
                W("tensor", s_w)
                if og - 1 >= 0:
                    W("tensor", s_se, gu_se[og - 1])
                for kc in range(KC):
                    for i in range(2):
                        O("tensor", lambda e, kc=kc, i=i, b=buf_g: e.matmul(
                            pf[i][:], b[:, 0:4096].bitcast(FP8)[
                                :, kc * 256 + i * 128: kc * 256 + (i + 1) * 128],
                            xn_fm[:, kc * TM:(kc + 1) * TM],
                            start=(kc == 0), stop=(kc == KC - 1)))
                    for i in range(2):
                        O("tensor", lambda e, kc=kc, i=i, b=buf_u: e.matmul(
                            pf[2 + i][:], b[:, 0:4096].bitcast(FP8)[
                                :, kc * 256 + i * 128: kc * 256 + (i + 1) * 128],
                            xn_fm[:, kc * TM:(kc + 1) * TM],
                            start=(kc == 0), stop=(kc == KC - 1)),
                            inc=(s_te, 1) if (kc == KC - 1 and i == 1) else None)
                gu_te[og] = s_te.n

            def guevac(og):
                W("scalar", s_te, gu_te[og])
                for i in range(2):
                    oc = og * 2 + i
                    W("scalar", s_ve)
                    O("scalar", lambda e, i=i, oc=oc: e.activation(
                        out=gatmp[:], in_=pf[i][:], func=AFT.Silu,
                        scale=gs_t[:, oc:oc + 1]))
                    O("scalar", lambda e, i=i, oc=oc: e.activation(
                        out=uptmp[:], in_=pf[2 + i][:], func=AFT.Identity,
                        scale=us_t[:, oc:oc + 1]), inc=(s_se, 1))
                    W("vector", s_se)
                    O("vector", lambda e, oc=oc: e.tensor_tensor(
                        out=fuse[:, oc * TM:(oc + 1) * TM], in0=gatmp[:], in1=uptmp[:],
                        op=ALU.mult), inc=(s_ve, 1))
                gu_se[og] = s_se.n

            guload(0)
            guchain(0)
            for og in range(24):
                if "gu" in SAFE:
                    em.barrier(bar, self_sems)
                if og + 1 < 24:
                    guload(og + 1)
                guevac(og)
                if og + 1 < 24:
                    guchain(og + 1)

            W("tensor", s_ve)
            d_te = {}
            d_w = {}

            def dload(gi):
                ocg, hk = divmod(gi, 6)
                buf = d8s[gi % 2]
                if gi - 2 >= 0:
                    W("sync", s_te, d_te[gi - 2])
                dsrc = scr["d8"][ocg, hk * 8:(hk + 1) * 8].rearrange("kc p o -> p kc o")
                O("sync", lambda e, a=dsrc, d=buf: e.dma_start(out=d[:], in_=a),
                  inc=(s_w, 16))
                d_w[gi] = s_w.n

            def dchain(gi):
                ocg, hk = divmod(gi, 6)
                buf = d8s[gi % 2]
                W("tensor", s_w, d_w[gi])
                if hk == 0:
                    W("tensor", s_se)
                    if "down4" in SAFE:
                        W("tensor", s_ve)
                        W("tensor", s_st)
                for kc8 in range(8):
                    kc = hk * 8 + kc8
                    for i in range(2):
                        O("tensor", lambda e, kc=kc, kc8=kc8, i=i, b=buf: e.matmul(
                            pf[i][:], b[:].bitcast(FP8)[
                                :, kc8 * 256 + i * 128: kc8 * 256 + (i + 1) * 128],
                            fuse[:, kc * TM:(kc + 1) * TM],
                            start=(kc == 0), stop=(kc == OC_FF - 1)),
                            inc=(s_te, 1) if (kc8 == 7 and i == 1) else None)
                d_te[gi] = s_te.n

            dload(0)
            for ocg in range(8):
                if "down" in SAFE:
                    em.barrier(bar, self_sems)
                if "down5" in SAFE:
                    W("sync", s_se)
                    W("sync", s_te)
                W("sync", s_st)
                W("sync", s_ve)
                for tb in range(4):
                    O("sync", lambda e, s=s, T0=T0, tb=tb, ocg=ocg: e.dma_start(
                        out=xh[tb][:, 0:256],
                        in_=din["x"][s, T0 + tb * 128:T0 + (tb + 1) * 128,
                                     ocg * 256:(ocg + 1) * 256]), inc=(s_x, 16))
                for hk in range(6):
                    gi = ocg * 6 + hk
                    if gi + 1 < 48:
                        dload(gi + 1)
                    dchain(gi)
                if "down1" in SAFE:
                    em.barrier(bar, self_sems)
                W("scalar", s_te)
                W("vector", s_x)
                for i in range(2):
                    oc = ocg * 2 + i
                    W("scalar", s_ve)
                    O("scalar", lambda e, i=i, oc=oc: e.activation(
                        out=gtmp[:], in_=pf[i][:], func=AFT.Silu,
                        scale=ds_t[:, oc:oc + 1]), inc=(s_se, 1))
                    W("vector", s_se)
                    O("vector", lambda e, oc=oc: e.tensor_tensor(
                        out=sgtmp[:], in0=gtmp[:], in1=s_fm[:, oc * TM:(oc + 1) * TM],
                        op=ALU.mult), inc=(s_ve, 1))
                    W("tensor", s_ve)
                    for tb in range(4):
                        O("tensor", lambda e, tb=tb, i=i: e.transpose(
                            pf[2 + tb][:, i * 128:(i + 1) * 128],
                            sgtmp[:, tb * 128:(tb + 1) * 128], identf_t[:]),
                            inc=(s_te, 1) if tb == 3 else None)
                    W("vector", s_te)
                if "down2" in SAFE:
                    em.barrier(bar, self_sems)
                for tb in range(4):
                    O("vector", lambda e, tb=tb: e.scalar_tensor_tensor(
                        out=xh[tb][:, 0:256], in0=pf[2 + tb][:, 0:256], scalar=1.0,
                        in1=xh[tb][:, 0:256], op0=ALU.mult, op1=ALU.add),
                        inc=(s_ve, 1))
                if "down3" in SAFE:
                    em.barrier(bar, self_sems)
                W("sync", s_ve)
                for tb in range(4):
                    O("sync", lambda e, s=s, T0=T0, tb=tb, ocg=ocg: e.dma_start(
                        out=y_out[s, T0 + tb * 128:T0 + (tb + 1) * 128,
                                  ocg * 256:(ocg + 1) * 256],
                        in_=xh[tb][:, 0:256]), inc=(s_st, 16))

        for s in range(nseq):
            ces = ExitStack()
            with ces:
                qp_sin = ces.enter_context(nc.sbuf_tensor(f"qp_sin{s}", [64, 16 * TM], BF16))
                qp_cos = ces.enter_context(nc.sbuf_tensor(f"qp_cos{s}", [64, 16 * TM], BF16))
                kp_sin = ces.enter_context(nc.sbuf_tensor(f"kp_sin{s}", [64, 8 * TM], BF16))
                kp_cos = ces.enter_context(nc.sbuf_tensor(f"kp_cos{s}", [64, 8 * TM], BF16))
                v_all = ces.enter_context(nc.sbuf_tensor(f"v_all{s}", [128, 8 * TM], BF16))
                vt_sb = ces.enter_context(nc.sbuf_tensor(f"vt_sb{s}", [128, 128], BF16))
                vt_sb2 = ces.enter_context(nc.sbuf_tensor(f"vt_sb2{s}", [128, 128], BF16))
                sm_sb = ces.enter_context(nc.sbuf_tensor(f"sm_sb{s}", [128, 256], BF16))
                sm_sb2 = ces.enter_context(nc.sbuf_tensor(f"sm_sb2{s}", [128, 256], BF16))
                xt = ces.enter_context(nc.sbuf_tensor(f"xt{s}", [128, HID], F32))
                scr2k = ces.enter_context(nc.sbuf_tensor(f"scr2k{s}", [128, HID], BF16))
                xn_tm = ces.enter_context(nc.sbuf_tensor(f"xn_tm{s}", [128, HID], BF16))
                lnst = ces.enter_context(nc.sbuf_tensor(f"lnst{s}", [128, 8], F32))
                qbf = ces.enter_context(nc.sbuf_tensor(f"qbf{s}", [128, TM], BF16))
                qrot = ces.enter_context(nc.sbuf_tensor(f"qrot{s}", [128, TM], BF16))
                t1f = ces.enter_context(nc.sbuf_tensor(f"t1f{s}", [128, TM], F32))
                sqf = ces.enter_context(nc.sbuf_tensor(f"sqf{s}", [128, TM], F32))
                t2f = sqf
                nbf = ces.enter_context(nc.sbuf_tensor(f"nbf{s}", [64, 1024], F32))
                rbf = ces.enter_context(nc.sbuf_tensor(f"rbf{s}", [64, 1024], F32))
                p_sb = ces.enter_context(nc.sbuf_tensor(f"p_sb{s}", [64, 1024], F32))
                pred = ces.enter_context(nc.sbuf_tensor(f"pred{s}", [64, 1024], F32))
                ki_sb = ces.enter_context(nc.sbuf_tensor(f"ki_sb{s}", [64, 1024], I32))
                ln = tuple(lnst[:, i:i + 1] for i in range(7))
                cbuf = (qp_sin, qp_cos, kp_sin, kp_cos, v_all, vt_sb, vt_sb2, sm_sb, sm_sb2, xt,
                        scr2k, xn_tm, ln, qbf, qrot, t1f, t2f, sqf, nbf, rbf,
                        p_sb, pred)
                phase_C(s, cbuf)
                em.barrier(bar, self_sems)
                if dbg and s == 0:
                    O("sync", lambda e: e.dma_start(out=dbg_t["dbg_am"], in_=am_sb[:]),
                      inc=(s_st, 16))
                    O("sync", lambda e: e.dma_start(out=dbg_t["dbg_qps"], in_=qp_sin[:]),
                      inc=(s_st, 16))
                    O("sync", lambda e: e.dma_start(out=dbg_t["dbg_kps"], in_=kp_sin[:]),
                      inc=(s_st, 16))
                    O("sync", lambda e: e.dma_start(out=dbg_t["dbg_vall"], in_=v_all[:]),
                      inc=(s_st, 16))
                    O("sync", lambda e: e.dma_start(out=dbg_t["dbg_xn"],
                                                    in_=scr["xn_s"][0]), inc=(s_st, 16))
                    O("sync", lambda e: e.dma_start(out=dbg_t["dbg_wqT"],
                                                    in_=scr["wqT"][0]), inc=(s_st, 16))
                    O("sync", lambda e: e.dma_start(out=dbg_t["dbg_pred"], in_=pred[:]),
                      inc=(s_st, 16))
                    O("sync", lambda e: e.dma_start(out=dbg_t["dbg_psb"], in_=p_sb[:]),
                      inc=(s_st, 16))
                    O("sync", lambda e: e.dma_start(out=dbg_t["dbg_rbf"], in_=rbf[:]),
                      inc=(s_st, 16))
                    O("sync", lambda e: e.dma_start(out=dbg_t["dbg_qrot"], in_=qrot[:]),
                      inc=(s_st, 16))
                    O("sync", lambda e: e.dma_start(out=dbg_t["dbg_sqf"], in_=sqf[:]),
                      inc=(s_st, 16))
                    O("sync", lambda e: e.dma_start(out=dbg_t["dbg_nbf"], in_=nbf[:]),
                      inc=(s_st, 16))
                    W("sync", s_st)
            bes = ExitStack()
            with bes:
                s_fm = bes.enter_context(nc.sbuf_tensor(f"s_fm{s}", [128, 16 * TM], BF16))
                fuse = bes.enter_context(nc.sbuf_tensor(f"fuse{s}", [128, OC_FF * TM], BF16))
                gu8a = bes.enter_context(nc.sbuf_tensor(f"gu8a{s}", [128, 4096], U8))
                gu8b = bes.enter_context(nc.sbuf_tensor(f"gu8b{s}", [128, 4096], U8))
                uu8a = bes.enter_context(nc.sbuf_tensor(f"uu8a{s}", [128, 4096], U8))
                uu8b = bes.enter_context(nc.sbuf_tensor(f"uu8b{s}", [128, 4096], U8))
                d8sa = bes.enter_context(nc.sbuf_tensor(f"d8sa{s}", [128, 2048], U8))
                d8sb = bes.enter_context(nc.sbuf_tensor(f"d8sb{s}", [128, 2048], U8))
                gtmp = bes.enter_context(nc.sbuf_tensor(f"gtmp{s}", [128, TM], F32))
                uptmp = bes.enter_context(nc.sbuf_tensor(f"uptmp{s}", [128, TM], BF16))
                gatmp = bes.enter_context(nc.sbuf_tensor(f"gatmp{s}", [128, TM], BF16))
                sgtmp = bes.enter_context(nc.sbuf_tensor(f"sgtmp{s}", [128, TM], F32))
                xh0 = bes.enter_context(nc.sbuf_tensor(f"xh0{s}", [128, 256], F32))
                xh1 = bes.enter_context(nc.sbuf_tensor(f"xh1{s}", [128, 256], F32))
                xh2 = bes.enter_context(nc.sbuf_tensor(f"xh2{s}", [128, 256], F32))
                xh3 = bes.enter_context(nc.sbuf_tensor(f"xh3{s}", [128, 256], F32))
                bbuf = (s_fm, fuse, (gu8a, gu8b), (uu8a, uu8b), (d8sa, d8sb),
                        gtmp, uptmp, gatmp, sgtmp, (xh0, xh1, xh2, xh3))
                for m in range(nmacro):
                    phase_W(s, m, s_fm)
                    em.barrier(bar, self_sems)
                    if dbg and s == 0 and m == 0:
                        O("sync", lambda e: e.dma_start(out=dbg_t["dbg_sfm"], in_=s_fm[:]),
                          inc=(s_st, 16))
                        W("sync", s_st)
                    if dbg and s == 0 and m == 3:
                        O("sync", lambda e: e.dma_start(out=dbg_t["dbg_sfm3"], in_=s_fm[:]),
                          inc=(s_st, 16))
                        W("sync", s_st)
                    phase_B(s, m, bbuf)
                    em.barrier(bar, self_sems)
                    if dbg and s == 0 and m == 0:
                        O("sync", lambda e: e.dma_start(out=dbg_t["dbg_fuse"], in_=fuse[:]),
                          inc=(s_st, 16))
                        W("sync", s_st)
                    if dbg and s == 0 and m == 3:
                        O("sync", lambda e: e.dma_start(out=dbg_t["dbg_fuse3"], in_=fuse[:]),
                          inc=(s_st, 16))
                        W("sync", s_st)

        W("sync", s_st)

        with nc.Block() as block:
            @block.sync
            def _(e):
                for f in em.prog["sync"]:
                    f(e)

            @block.vector
            def _(e):
                for f in em.prog["vector"]:
                    f(e)

            @block.scalar
            def _(e):
                for f in em.prog["scalar"]:
                    f(e)

            @block.tensor
            def _(e):
                for f in em.prog["tensor"]:
                    f(e)

    return nc


_NC_CACHE = {}


def kernel(**inputs):
    consts = _consts()
    if "nc" not in _NC_CACHE:
        _NC_CACHE["nc"] = build_nc()
    nc = _NC_CACHE["nc"]

    x = np.ascontiguousarray(inputs["L0_attn_hidden_states"])
    in_maps = []
    for c in range(NCORES):
        m = {
            "x": x[c * NSEQ:(c + 1) * NSEQ],
            "wq": inputs["L0_attn_Wq"], "bq": inputs["L0_attn_bq"],
            "wk": inputs["L0_attn_Wk"], "bk": inputs["L0_attn_bk"],
            "wv": inputs["L0_attn_Wv"], "bv": inputs["L0_attn_bv"],
            "wo": inputs["L0_attn_Wo"], "bo": inputs["L0_attn_bo"],
            "rmat": inputs["L0_attn_rand_matrix"],
            "gw": inputs["L0_ffn_gate_w"], "uw": inputs["L0_ffn_up_w"],
            "dw": inputs["L0_ffn_down_w"],
            "gs": inputs["L0_ffn_gate_scale"], "us": inputs["L0_ffn_up_scale"],
            "ds": inputs["L0_ffn_down_scale"],
        }
        m.update(consts)
        in_maps.append(m)

    res = run_bass_kernel_spmd(nc, in_maps, core_ids=list(range(NCORES)))
    out = np.concatenate([res.results[c]["y"] for c in range(NCORES)], axis=0)
    return np.ascontiguousarray(out.astype(np.float32))


# version 09
# speedup vs baseline: 1.0471x; 1.0471x over previous
import math
import os
from contextlib import ExitStack

import numpy as np
import ml_dtypes

import concourse.bass as bass
import concourse.mybir as mybir
from concourse.bass_utils import run_bass_kernel_spmd

F32 = mybir.dt.float32
BF16 = mybir.dt.bfloat16
FP8 = mybir.dt.float8e4
U8 = mybir.dt.uint8
I32 = mybir.dt.int32
ALU = mybir.AluOpType
AFT = mybir.ActivationFunctionType

HID = 2048
SEQ = 2048
HQ, HKV, D = 16, 8, 128
I_FF = 6144
NCORES = 8
NSEQ = 2
NMACRO = 4
TM = 512
NTB = TM // 128
NBLK = TM // 16
KC = HID // 128
OC_FF = I_FF // 128

PI = math.pi
C1 = float(np.float32(2 * PI))
C2 = float(np.float32(np.float64(2 * PI) - np.float64(np.float32(2 * PI))))
INV2PI = float(np.float32(1.0 / (2 * PI)))
LN_EPS = 1e-5
MAGIC = 12582912.0
SAFE = set(os.environ.get('KSAFE', 'down4').split(','))


class Sem:
    def __init__(self, nc, es, name):
        self.h = es.enter_context(nc.semaphore(name))
        self.n = 0


class Emitter:
    def __init__(self):
        self.prog = {e: [] for e in ("sync", "vector", "scalar", "tensor")}

    def op(self, eng, fn, inc=None):
        if inc is not None:
            sem, k = inc
            def run(e, fn=fn, sem=sem, k=k):
                fn(e).then_inc(sem.h, k)
            sem.n += k
        else:
            run = fn
        self.prog[eng].append(run)

    def wait(self, eng, sem, val=None):
        v = sem.n if val is None else val
        if v > 0:
            self.prog[eng].append(lambda e, s=sem, v=v: e.wait_ge(s.h, v))

    def barrier(self, bar, self_sems=None):
        engines = ("sync", "vector", "scalar", "tensor")
        if self_sems:
            for e in engines:
                for s in self_sems.get(e, ()):
                    self.wait(e, s)
        for e in engines:
            self.op(e, lambda x: x.nop(), inc=(bar[e], 1))
        for e in engines:
            for o in engines:
                if o != e:
                    self.wait(e, bar[o])


def _rope_tables():
    import jax
    cpu = jax.devices("cpu")[0]
    import jax.numpy as jnp
    with jax.default_device(cpu):
        inv_freq = 1.0 / 10000.0 ** (jnp.arange(0, D, 2, dtype=jnp.float32) / D)
        freqs = jnp.arange(SEQ, dtype=jnp.float32)[:, None] * inv_freq[None, :]
        emb = jnp.concatenate((freqs, freqs), axis=-1)
        cos = np.asarray(jnp.cos(emb))
        sin = np.asarray(jnp.sin(emb))
    cosT = np.ascontiguousarray(cos.T).astype(np.float32)
    sinT = np.ascontiguousarray(sin.T).astype(np.float32)
    return cosT, sinT


def _consts():
    cosT, sinT = _rope_tables()
    rotT = np.zeros((128, 128), np.float32)
    for d in range(64):
        rotT[d + 64, d] = -1.0
        rotT[d, d + 64] = 1.0
    maskM = np.zeros((128, 256), np.float32)
    for tau in range(16):
        for j in range(8):
            for h in range(16):
                maskM[tau * 8 + j, tau * 16 + h] = 1.0 / 64.0
    return {
        "cosT": cosT,
        "sinSS": sinT,
        "rotT": rotT.astype(ml_dtypes.bfloat16),
        "maskM": maskM.astype(ml_dtypes.bfloat16),
        "identb": np.eye(128, dtype=ml_dtypes.bfloat16),
        "identf": np.eye(128, dtype=np.float32),
    }


def build_nc(nseq=NSEQ, nmacro=NMACRO, dbg=False):
    nc = bass.Bass("TRN2", target_bir_lowering=False, detect_race_conditions=False)
    es = ExitStack()

    din = {}
    for name, shp, dt in [
        ("x", [nseq, SEQ, HID], F32),
        ("wq", [HID, HID], F32), ("bq", [HID], F32),
        ("wk", [HKV * D, HID], F32), ("bk", [HKV * D], F32),
        ("wv", [HKV * D, HID], F32), ("bv", [HKV * D], F32),
        ("wo", [HID, HID], F32), ("bo", [HID], F32),
        ("rmat", [64, 128], F32),
        ("gw", [HID, I_FF], F32), ("uw", [HID, I_FF], F32), ("dw", [I_FF, HID], F32),
        ("gs", [I_FF], F32), ("us", [I_FF], F32), ("ds", [HID], F32),
        ("cosT", [128, SEQ], F32), ("sinSS", [128, SEQ], F32),
        ("rotT", [128, 128], BF16), ("maskM", [128, 256], BF16),
        ("identb", [128, 128], BF16), ("identf", [128, 128], F32),
    ]:
        din[name] = nc.dram_tensor(name, shp, dt, kind="ExternalInput").ap()
    y_out = nc.dram_tensor("y", [nseq, SEQ, HID], F32, kind="ExternalOutput").ap()
    dbg_t = {}
    if dbg:
        for nm, shp, dt in [("dbg_am", [128, 16 * SEQ], BF16),
                            ("dbg_sfm", [128, 16 * TM], BF16),
                            ("dbg_fuse", [128, OC_FF * TM], BF16),
                            ("dbg_xn", [KC, 128, SEQ], BF16),
                            ("dbg_wqT", [KC, 128, 128], BF16),
                            ("dbg_qps", [64, 16 * TM], BF16),
                            ("dbg_kps", [64, 8 * TM], BF16),
                            ("dbg_vall", [128, 8 * TM], BF16),
                            ("dbg_pred", [64, 1024], F32),
                            ("dbg_psb", [64, 1024], F32),
                            ("dbg_rbf", [64, 1024], F32),
                            ("dbg_qrot", [128, TM], BF16),
                            ("dbg_sqf", [128, TM], F32),
                            ("dbg_nbf", [64, 1024], F32),
                            ("dbg_sfm3", [128, 16 * TM], BF16),
                            ("dbg_fuse3", [128, OC_FF * TM], BF16),
                            ("dbg_g3", [128, TM], F32)]:
            dbg_t[nm] = nc.dram_tensor(nm, shp, dt, kind="ExternalOutput").ap()

    scr = {
        "wq_nat": nc.dram_tensor("wq_nat", [HID, HID], BF16, kind="Internal").ap(),
        "wk_nat": nc.dram_tensor("wk_nat", [HKV * D, HID], BF16, kind="Internal").ap(),
        "wv_nat": nc.dram_tensor("wv_nat", [HKV * D, HID], BF16, kind="Internal").ap(),
        "wo_nat": nc.dram_tensor("wo_nat", [HID, HID], BF16, kind="Internal").ap(),
        "wqT": nc.dram_tensor("wqT", [16, KC, 128, 128], BF16, kind="Internal").ap(),
        "wkT": nc.dram_tensor("wkT", [8, KC, 128, 128], BF16, kind="Internal").ap(),
        "wvT": nc.dram_tensor("wvT", [8, KC, 128, 128], BF16, kind="Internal").ap(),
        "woT": nc.dram_tensor("woT", [16, KC, 128, 128], BF16, kind="Internal").ap(),
        "g8": nc.dram_tensor("g8", [24, 128, KC, 256], U8, kind="Internal").ap(),
        "u8": nc.dram_tensor("u8", [24, 128, KC, 256], U8, kind="Internal").ap(),
        "d8": nc.dram_tensor("d8", [8, 6, 128, 8, 256], U8, kind="Internal").ap(),
        "xn_s": nc.dram_tensor("xn_s", [nseq, KC, 128, SEQ], BF16, kind="Internal").ap(),
    }

    em = Emitter()
    O, W = em.op, em.wait

    es.enter_context(nc.allow_low_precision(
        "reference quantizes FFN weights to fp8-e4m3; bf16 matmul operands with "
        "fp32 accumulation are within this problem's stated tolerance"))
    with es:
        sb = lambda name, shp, dt: es.enter_context(nc.sbuf_tensor(name, shp, dt))
        S = lambda name: Sem(nc, es, name)

        cos_t = sb("cos_t", [128, SEQ], F32)
        sinss_t = sb("sinss_t", [128, SEQ], F32)
        rot_t = sb("rot_t", [128, 128], BF16)
        mask_t = sb("mask_t", [128, 256], BF16)
        identb_t = sb("identb_t", [128, 128], BF16)
        identf_t = sb("identf_t", [128, 128], F32)
        ones64 = sb("ones64", [128, 64], F32)
        rT_t = sb("rT_t", [128, 64], BF16)
        bq_t = sb("bq_t", [128, 16], F32)
        bk_t = sb("bk_t", [128, 8], F32)
        bv_t = sb("bv_t", [128, 8], F32)
        bo_t = sb("bo_t", [128, 16], F32)
        gs_t = sb("gs_t", [128, 48], F32)
        us_t = sb("us_t", [128, 48], F32)
        ds_t = sb("ds_t", [128, 16], F32)
        am_sb = sb("am_sb", [128, 16 * SEQ], BF16)
        xn_fm = sb("xn_fm", [128, KC * TM], BF16)
        wslab = [sb(f"wslab{i}", [128, 2048], BF16) for i in range(2)]

        pf = [es.enter_context(nc.psum_tensor(f"pf{i}", [128, 512], F32))
              for i in range(8)]
        pbf = pf[3][:].bitcast(BF16)

        s_ld = S("s_ld")
        s_x = S("s_x")
        s_w = S("s_w")
        s_sp = S("s_sp")
        s_te = S("s_te")
        s_ve = S("s_ve")
        s_se = S("s_se")
        s_st = S("s_st")
        bar = {e: S("bar_" + e) for e in ("sync", "vector", "scalar", "tensor")}
        self_sems = {"sync": (s_ld, s_x, s_w, s_sp, s_st), "vector": (s_ve,),
                     "scalar": (s_se,), "tensor": (s_te,)}

        for src, dst in [(din["cosT"], cos_t), (din["sinSS"], sinss_t),
                         (din["rotT"], rot_t), (din["maskM"], mask_t),
                         (din["identb"], identb_t), (din["identf"], identf_t)]:
            O("sync", lambda e, s=src, d=dst: e.dma_start(out=d[:], in_=s), inc=(s_ld, 16))
        O("vector", lambda e: e.memset(ones64[:], 1.0))
        O("vector", lambda e: e.memset(am_sb[:], 0.0), inc=(s_ve, 1))

        btmp = sb("btmp", [64, 128], F32)
        bias_specs = [(din["bq"], 16, bq_t), (din["bk"], 8, bk_t), (din["bv"], 8, bv_t),
                      (din["bo"], 16, bo_t), (din["gs"], 48, gs_t), (din["us"], 48, us_t),
                      (din["ds"], 16, ds_t)]
        W("sync", s_ld)
        for bsrc, n, bdst in bias_specs:
            W("sync", s_te)
            src_ap = bsrc.rearrange("(a b) -> a b", b=128)
            O("sync", lambda e, s=src_ap, n=n: e.dma_start(out=btmp[0:n, :], in_=s),
              inc=(s_ld, 16))
            W("tensor", s_ld)
            W("tensor", s_se)
            O("tensor", lambda e, n=n: e.transpose(pf[0][:, 0:n], btmp[0:n, :],
                                                   identf_t[0:n, 0:n]), inc=(s_te, 1))
            W("scalar", s_te)
            O("scalar", lambda e, n=n, d=bdst: e.copy(out=d[:, 0:n], in_=pf[0][:, 0:n]),
              inc=(s_se, 1))
        W("sync", s_te)
        O("sync", lambda e: e.dma_start(out=btmp[0:64, :], in_=din["rmat"]), inc=(s_ld, 16))
        W("tensor", s_ld)
        W("tensor", s_se)
        O("tensor", lambda e: e.transpose(pf[0][:, 0:64], btmp[0:64, :],
                                          identf_t[0:64, 0:64]), inc=(s_te, 1))
        W("scalar", s_te)
        O("scalar", lambda e: e.copy(out=rT_t[:], in_=pf[0][:, 0:64]), inc=(s_se, 1))

        with nc.sbuf_tensor("w32a", [128, I_FF], F32) as w32a, \
             nc.sbuf_tensor("w32b", [128, I_FF], F32) as w32b, \
             nc.sbuf_tensor("wbfa", [128, I_FF // 2], BF16) as wbfa, \
             nc.sbuf_tensor("wbfb", [128, I_FF // 2], BF16) as wbfb, \
             nc.sbuf_tensor("w8a", [128, I_FF], U8) as w8a, \
             nc.sbuf_tensor("w8b", [128, I_FF], U8) as w8b:
            w32x = [w32a, w32b]
            wbfx = [wbfa, wbfb]
            w8x = [w8a, w8b]
            ld_rec, cv_rec, sp_rec = {}, {}, {}

            jobs = []
            for wsrc, wnat, nrow in [(din["wq"], scr["wq_nat"], 16),
                                     (din["wk"], scr["wk_nat"], 8),
                                     (din["wv"], scr["wv_nat"], 8),
                                     (din["wo"], scr["wo_nat"], 16)]:
                for r in range(nrow):
                    jobs.append(("attn", wsrc, wnat, r))
            for wsrc, wdst in [(din["gw"], scr["g8"]), (din["uw"], scr["u8"])]:
                for r in range(KC):
                    jobs.append(("guq", wsrc, wdst, r))
            for r in range(OC_FF):
                jobs.append(("dwq", din["dw"], scr["d8"], r))

            def p0_load(j):
                kind, wsrc, _, r = jobs[j]
                w32 = w32x[j % 2]
                if j - 2 >= 0:
                    W("sync", s_ve, cv_rec[j - 2])
                ncols = HID if kind != "guq" else I_FF
                O("sync", lambda e, s=wsrc, r=r, w32=w32, nc_=ncols: e.dma_start(
                    out=w32[:, 0:nc_], in_=s[r * 128:(r + 1) * 128, :]), inc=(s_ld, 16))
                ld_rec[j] = s_ld.n

            def p0_conv(j):
                kind, _, _, r = jobs[j]
                w32 = w32x[j % 2]
                W("vector", s_ld, ld_rec[j])
                if j - 2 >= 0:
                    W("vector", s_sp, sp_rec[j - 2])
                if kind == "attn":
                    out = wbfx[j % 2]
                    O("vector", lambda e, o=out, w32=w32: e.tensor_copy(
                        out=o[:, 0:HID], in_=w32[:, 0:HID]), inc=(s_ve, 1))
                elif kind == "guq":
                    out = w8x[j % 2]
                    O("vector", lambda e, o=out, w32=w32: e.tensor_copy(
                        out=o[:].bitcast(FP8), in_=w32[:]), inc=(s_ve, 1))
                else:
                    out = w8x[j % 2]
                    O("vector", lambda e, o=out, w32=w32: e.tensor_copy(
                        out=o[:, 0:HID].bitcast(FP8), in_=w32[:, 0:HID]), inc=(s_ve, 1))
                cv_rec[j] = s_ve.n

            def p0_store(j):
                kind, _, wdst, r = jobs[j]
                W("scalar", s_ve, cv_rec[j])
                if kind == "attn":
                    out = wbfx[j % 2]
                    O("scalar", lambda e, d=wdst, r=r, o=out: e.dma_start(
                        out=d[r * 128:(r + 1) * 128, :], in_=o[:, 0:HID]), inc=(s_sp, 16))
                elif kind == "guq":
                    out = w8x[j % 2]
                    dst_ap = wdst[:, :, r].rearrange("og p o -> p og o")
                    O("scalar", lambda e, d=dst_ap, o=out: e.dma_start(
                        out=d, in_=o[:]), inc=(s_sp, 16))
                else:
                    out = w8x[j % 2]
                    dst_ap = wdst[:, r // 8, :, r % 8].rearrange("g p o -> p g o")
                    O("scalar", lambda e, d=dst_ap, o=out: e.dma_start(
                        out=d, in_=o[:, 0:HID]), inc=(s_sp, 16))
                sp_rec[j] = s_sp.n

            p0_load(0)
            for j in range(len(jobs)):
                if j + 1 < len(jobs):
                    p0_load(j + 1)
                p0_conv(j)
                p0_store(j)

            W("sync", s_sp)
            tr_rec, ts_rec = {}, {}
            tjobs = []
            for wnat, wT, noc in [(scr["wq_nat"], scr["wqT"], 16),
                                  (scr["wk_nat"], scr["wkT"], 8),
                                  (scr["wv_nat"], scr["wvT"], 8),
                                  (scr["wo_nat"], scr["woT"], 16)]:
                for kc in range(KC):
                    tjobs.append((wnat, wT, noc, kc))

            def t_load(j):
                wnat, wT, noc, kc = tjobs[j]
                buf = wbfx[j % 2]
                if j - 2 >= 0:
                    W("sync", s_sp, ts_rec[j - 2])
                O("sync", lambda e, s=wnat, kc=kc, nr=noc * 128, b=buf: e.dma_start(
                    out=b[:, 0:nr], in_=s[:, kc * 128:(kc + 1) * 128],
                    transpose=True), inc=(s_ld, 16))
                tr_rec[j] = s_ld.n

            def t_store(j):
                wnat, wT, noc, kc = tjobs[j]
                buf = wbfx[j % 2]
                W("scalar", s_ld, tr_rec[j])
                dst_ap = wT[:, kc].rearrange("oc p o -> p oc o")
                O("scalar", lambda e, d=dst_ap, n=noc * 128, b=buf: e.dma_start(
                    out=d, in_=b[:, 0:n]), inc=(s_sp, 16))
                ts_rec[j] = s_sp.n

            t_load(0)
            for j in range(len(tjobs)):
                if j + 1 < len(tjobs):
                    t_load(j + 1)
                t_store(j)
        em.barrier(bar, self_sems)
        em.barrier(bar, self_sems)

        def phase_C(s, cbuf):
            (qp_sin, qp_cos, kp_sin, kp_cos, v_all, vt_sb, vt_sb2, sm_sb, sm_sb2, xt, scr2k, xn_tm,
             ln, qbf, qrot, t1f, t2f, sqf, nbf, rbf, p_sb, pred) = cbuf
            ln_sx, ln_sq, ln_mu, ln_t0, ln_t1, ln_rstd, ln_nb = ln
            for m in range(nmacro):
                T0 = m * TM
                for tb in range(NTB):
                    W("sync", s_se)
                    O("sync", lambda e, s=s, t=T0 + tb * 128: e.dma_start(
                        out=xt[:], in_=din["x"][s, t:t + 128, :]), inc=(s_x, 16))
                    W("vector", s_x)
                    O("vector", lambda e: e.tensor_scalar(
                        out=scr2k[:], in0=xt[:], scalar1=1.0, scalar2=None,
                        op0=ALU.mult, op1=ALU.add, accum_out=ln_sx[:]), inc=(s_ve, 1))
                    W("scalar", s_x)
                    W("scalar", s_ve)
                    O("scalar", lambda e: e.activation(
                        out=scr2k[:], in_=xt[:], func=AFT.Square,
                        accum_out=ln_sq[:]), inc=(s_se, 1))
                    W("vector", s_se)
                    O("vector", lambda e: e.tensor_scalar(
                        out=ln_mu[:], in0=ln_sx[:], scalar1=1.0 / HID, scalar2=None,
                        op0=ALU.mult))
                    O("vector", lambda e: e.tensor_scalar(
                        out=ln_t1[:], in0=ln_sq[:], scalar1=1.0 / HID, scalar2=None,
                        op0=ALU.mult))
                    O("vector", lambda e: e.scalar_tensor_tensor(
                        out=ln_t0[:], in0=ln_mu[:], scalar=1.0, in1=ln_mu[:],
                        op0=ALU.mult, op1=ALU.mult))
                    O("vector", lambda e: e.scalar_tensor_tensor(
                        out=ln_t1[:], in0=ln_t0[:], scalar=-1.0, in1=ln_t1[:],
                        op0=ALU.mult, op1=ALU.add))
                    O("vector", lambda e: e.tensor_scalar(
                        out=ln_t1[:], in0=ln_t1[:], scalar1=1.0, scalar2=LN_EPS,
                        op0=ALU.mult, op1=ALU.add), inc=(s_ve, 1))
                    W("scalar", s_ve)
                    O("scalar", lambda e: e.activation(out=ln_t1[:], in_=ln_t1[:],
                                                       func=AFT.Ln))
                    O("scalar", lambda e: e.activation(out=ln_rstd[:], in_=ln_t1[:],
                                                       func=AFT.Exp, scale=np.float32(-0.5)),
                      inc=(s_se, 1))
                    W("vector", s_se)
                    O("vector", lambda e: e.scalar_tensor_tensor(
                        out=ln_nb[:], in0=ln_mu[:], scalar=-1.0, in1=ln_rstd[:],
                        op0=ALU.mult, op1=ALU.mult), inc=(s_ve, 1))
                    W("scalar", s_ve)
                    O("scalar", lambda e: e.activation(
                        out=xn_tm[:], in_=xt[:], func=AFT.Identity, scale=ln_rstd[:],
                        bias=ln_nb[:]), inc=(s_se, 1))
                    W("tensor", s_se)
                    for g in range(4):
                        W("tensor", s_se)
                        for i in range(4):
                            kc = g * 4 + i
                            O("tensor", lambda e, kc=kc, i=i: e.transpose(
                                pbf[:, i * 128:(i + 1) * 128],
                                xn_tm[:, kc * 128:(kc + 1) * 128], identb_t[:]),
                                inc=(s_te, 1) if i == 3 else None)
                        W("scalar", s_te)
                        dst = xn_fm[:].rearrange("p (kc t) -> p kc t", kc=KC)[
                            :, g * 4:(g + 1) * 4, tb * 128:(tb + 1) * 128]
                        O("scalar", lambda e, d=dst: e.activation(
                            out=d, in_=pbf[:, 0:512], func=AFT.Copy), inc=(s_se, 1))
                W("sync", s_se)
                dst_ap = scr["xn_s"][s, :, :, T0:T0 + TM].rearrange("kc p t -> p kc t")
                O("sync", lambda e, a=dst_ap: e.dma_start(
                    out=a, in_=xn_fm[:]), inc=(s_sp, 16))

                heads = [("q", h) for h in range(HQ)] + \
                        [("k", h) for h in range(HKV)] + \
                        [("v", h) for h in range(HKV)]
                te_after = {}
                se_after = {}

                def qkv_load(i):
                    proj, h = heads[i]
                    wT = {"q": scr["wqT"], "k": scr["wkT"], "v": scr["wvT"]}[proj]
                    buf = wslab[i % 2]
                    if i - 2 >= 0:
                        W("scalar", s_te, te_after[i - 2])
                    src_ap = wT[h].rearrange("kc p o -> p kc o")
                    O("scalar", lambda e, a=src_ap, b=buf: e.dma_start(
                        out=b[:, 0:2048], in_=a), inc=(s_w, 16))

                def qkv_chain(i):
                    proj, h = heads[i]
                    buf = wslab[i % 2]
                    pacc = pf[i % 2]
                    W("tensor", s_w)
                    if i - 2 >= 0:
                        W("tensor", s_se, se_after[i - 2])
                    else:
                        W("tensor", s_se)
                    for kc in range(KC):
                        O("tensor", lambda e, b=buf, kc=kc, p=pacc: e.matmul(
                            p[:], b[:, kc * 128:(kc + 1) * 128],
                            xn_fm[:, kc * TM:(kc + 1) * TM],
                            start=(kc == 0), stop=(kc == KC - 1)),
                            inc=(s_te, 1) if kc == KC - 1 else None)
                    te_after[i] = s_te.n

                def qkv_evac(i):
                    proj, h = heads[i]
                    pacc = pf[i % 2]
                    W("scalar", s_te, te_after[i])
                    if proj == "v":
                        dst = v_all[:].rearrange("p (t j) -> p t j", j=8)[:, :, h]
                        O("scalar", lambda e, d=dst, p=pacc, h=h: e.activation(
                            out=d, in_=p[:], func=AFT.Identity, bias=bv_t[:, h:h + 1]),
                            inc=(s_se, 1))
                    else:
                        bias = {"q": bq_t, "k": bk_t}[proj]
                        W("scalar", s_ve)
                        O("scalar", lambda e, p=pacc, h=h, b=bias: e.activation(
                            out=qbf[:], in_=p[:], func=AFT.Identity, bias=b[:, h:h + 1]),
                            inc=(s_se, 1))
                    se_after[i] = s_se.n

                def qkv_epi(i):
                    proj, h = heads[i]
                    if proj == "v":
                        return
                    W("tensor", s_se, se_after[i])
                    W("tensor", s_ve)
                    O("tensor", lambda e: e.matmul(pf[2][:], rot_t[:], qbf[:],
                                                   start=True, stop=True), inc=(s_te, 1))
                    W("vector", s_te)
                    O("vector", lambda e, T0=T0: e.tensor_tensor(
                        out=t1f[:], in0=qbf[:], in1=cos_t[:, T0:T0 + TM], op=ALU.mult))
                    O("vector", lambda e, T0=T0: e.tensor_tensor(
                        out=t2f[:], in0=pf[2][:], in1=sinss_t[:, T0:T0 + TM],
                        op=ALU.mult))
                    O("vector", lambda e: e.tensor_tensor(
                        out=qrot[:], in0=t1f[:], in1=t2f[:], op=ALU.add), inc=(s_ve, 1))
                    W("scalar", s_ve)
                    O("scalar", lambda e: e.activation(
                        out=sqf[:], in_=qrot[:], func=AFT.Square), inc=(s_se, 1))
                    W("tensor", s_se)
                    W("tensor", s_ve)
                    pn = pf[6 + (h % 2)]
                    O("tensor", lambda e, pn=pn: e.matmul(
                        pn[0:64, :], ones64[:, 0:64], sqf[:],
                        start=True, stop=True))
                    pq = pf[4 + (h % 2)]
                    O("tensor", lambda e, pq=pq: e.matmul(
                        pq[0:64, :], rT_t[:], qrot[:], start=True, stop=True),
                        inc=(s_te, 1))
                    if h % 2 == 1:
                        featmap_pack(proj, h - 1, 16 if proj == "q" else 8)

                def featmap_pack(proj, hA, stride):
                    sink = qp_sin if proj == "q" else kp_sin
                    cosk = qp_cos if proj == "q" else kp_cos
                    W("vector", s_te)
                    O("vector", lambda e: e.tensor_scalar(
                        out=nbf[:, 0:512], in0=pf[6][0:64, :], scalar1=1e-12,
                        scalar2=None, op0=ALU.max))
                    O("vector", lambda e: e.tensor_scalar(
                        out=nbf[:, 512:1024], in0=pf[7][0:64, :], scalar1=1e-12,
                        scalar2=None, op0=ALU.max), inc=(s_ve, 1))
                    W("scalar", s_ve)
                    O("scalar", lambda e: e.activation(out=nbf[:], in_=nbf[:],
                                                       func=AFT.Ln))
                    O("scalar", lambda e: e.activation(out=rbf[:], in_=nbf[:],
                                                       func=AFT.Exp, scale=np.float32(-0.5)),
                      inc=(s_se, 1))
                    W("vector", s_se)
                    O("vector", lambda e: e.tensor_tensor(
                        out=p_sb[:, 0:512], in0=pf[4][0:64, :], in1=rbf[:, 0:512],
                        op=ALU.mult))
                    O("vector", lambda e: e.tensor_tensor(
                        out=p_sb[:, 512:1024], in0=pf[5][0:64, :], in1=rbf[:, 512:1024],
                        op=ALU.mult))
                    kft = nbf
                    O("vector", lambda e: e.tensor_scalar(
                        out=ki_sb[:].bitcast(F32), in0=p_sb[:], scalar1=INV2PI,
                        scalar2=MAGIC, op0=ALU.mult, op1=ALU.add))
                    O("vector", lambda e, kft=kft: e.tensor_scalar(
                        out=kft[:], in0=ki_sb[:].bitcast(F32), scalar1=1.0,
                        scalar2=-MAGIC, op0=ALU.mult, op1=ALU.add))
                    O("vector", lambda e, kft=kft: e.scalar_tensor_tensor(
                        out=pred[:], in0=kft[:], scalar=-C1, in1=p_sb[:],
                        op0=ALU.mult, op1=ALU.add))
                    O("vector", lambda e: e.tensor_scalar(
                        out=pred[:], in0=pred[:], scalar1=PI, scalar2=-PI,
                        op0=ALU.min, op1=ALU.max), inc=(s_ve, 1))
                    W("scalar", s_ve)
                    sdst = sink[:].rearrange("p (t s) -> p t s", s=stride)[
                        :, :, hA:hA + 2].transpose([0, 2, 1])
                    cdst = cosk[:].rearrange("p (t s) -> p t s", s=stride)[
                        :, :, hA:hA + 2].transpose([0, 2, 1])
                    psrc = pred[:].rearrange("p (a t) -> p a t", a=2)
                    O("scalar", lambda e, d=sdst, s2=psrc: e.activation(
                        out=d, in_=s2, func=AFT.Sin))
                    O("scalar", lambda e: e.activation(out=p_sb[:], in_=pred[:],
                                                       func=AFT.Abs))
                    pasrc = p_sb[:].rearrange("p (a t) -> p a t", a=2)
                    O("scalar", lambda e, d=cdst, s2=pasrc: e.activation(
                        out=d, in_=s2, func=AFT.Sin, scale=np.float32(-1.0), bias=np.float32(PI / 2)),
                        inc=(s_se, 1))

                qkv_load(0)
                qkv_chain(0)
                for i in range(len(heads)):
                    if "qkv" in SAFE:
                        em.barrier(bar, self_sems)
                    if i + 1 < len(heads):
                        qkv_load(i + 1)
                    qkv_evac(i)
                    if i + 1 < len(heads):
                        qkv_chain(i + 1)
                    qkv_epi(i)

                W("tensor", s_se)
                W("tensor", s_ve)
                bl_mask_ve = {}
                bl_vt_se = {}
                bl_am_se = {}
                for b in range(NBLK):
                    if "bil" in SAFE:
                        em.barrier(bar, self_sems)
                    Babs = m * NBLK + b
                    par = b % 2
                    pS = pf[2] if par == 0 else pf[6]
                    pO = pf[4] if par == 0 else pf[5]
                    pV = pbf if par == 0 else pf[7][:].bitcast(BF16)
                    vt = vt_sb if par == 0 else vt_sb2
                    sm = sm_sb if par == 0 else sm_sb2
                    if b - 2 >= 0:
                        W("tensor", s_ve, bl_mask_ve[b - 2])
                    O("tensor", lambda e, b=b, pS=pS: e.matmul(
                        pS[:, 0:256], kp_sin[:, b * 128:(b + 1) * 128],
                        qp_sin[:, b * 256:(b + 1) * 256], start=True, stop=False))
                    O("tensor", lambda e, b=b, pS=pS: e.matmul(
                        pS[:, 0:256], kp_cos[:, b * 128:(b + 1) * 128],
                        qp_cos[:, b * 256:(b + 1) * 256], start=False, stop=True),
                        inc=(s_te, 1))
                    sK = s_te.n
                    if b - 2 >= 0:
                        W("tensor", s_se, bl_vt_se[b - 2])
                    O("tensor", lambda e, b=b, pV=pV: e.transpose(
                        pV[:, 0:128], v_all[:, b * 128:(b + 1) * 128], identb_t[:]),
                        inc=(s_te, 1))
                    tV = s_te.n
                    W("vector", s_te, sK)
                    O("vector", lambda e, pS=pS, sm=sm: e.tensor_tensor(
                        out=sm[:], in0=pS[:, 0:256], in1=mask_t[:], op=ALU.mult),
                        inc=(s_ve, 1))
                    bl_mask_ve[b] = s_ve.n
                    W("scalar", s_te, tV)
                    O("scalar", lambda e, pV=pV, vt=vt: e.copy(out=vt[:], in_=pV[:, 0:128]),
                      inc=(s_se, 1))
                    bl_vt_se[b] = s_se.n
                    W("tensor", s_ve, bl_mask_ve[b])
                    W("tensor", s_se, bl_vt_se[b])
                    if b - 2 >= 0:
                        W("tensor", s_se, bl_am_se[b - 2])
                    O("tensor", lambda e, vt=vt, sm=sm, pO=pO: e.matmul(
                        pO[:, 0:256], vt[:], sm[:], start=True, stop=True), inc=(s_te, 1))
                    tO = s_te.n
                    W("scalar", s_te, tO)
                    adst = am_sb[:].rearrange("p (jj h t) -> p jj h t", jj=16, h=16)[
                        :, :, :, Babs]
                    asrc = pO[:, 0:256].rearrange("p (jj h) -> p jj h", jj=16)
                    O("scalar", lambda e, d=adst, s2=asrc: e.activation(
                        out=d, in_=s2, func=AFT.Copy), inc=(s_se, 1))
                    bl_am_se[b] = s_se.n

        def phase_W(s, m, s_fm):
            w_te = {}
            w_se = {}

            def wload(oc):
                buf = wslab[oc % 2]
                if oc - 2 >= 0:
                    W("scalar", s_te, w_te[oc - 2])
                src_ap = scr["woT"][oc].rearrange("kc p o -> p kc o")
                O("scalar", lambda e, a=src_ap, b=buf: e.dma_start(
                    out=b[:, 0:2048], in_=a), inc=(s_w, 16))

            def wchain(oc):
                buf = wslab[oc % 2]
                pacc = pf[oc % 2]
                W("tensor", s_w)
                if oc - 2 >= 0:
                    W("tensor", s_se, w_se[oc - 2])
                for jj in range(16):
                    O("tensor", lambda e, jj=jj, b=buf, p=pacc: e.matmul(
                        p[:], b[:, jj * 128:(jj + 1) * 128],
                        am_sb[:, jj * SEQ + m * TM: jj * SEQ + (m + 1) * TM],
                        start=(jj == 0), stop=(jj == 15)),
                        inc=(s_te, 1) if jj == 15 else None)
                w_te[oc] = s_te.n

            def wevac(oc):
                pacc = pf[oc % 2]
                W("scalar", s_te, w_te[oc])
                O("scalar", lambda e, oc=oc, p=pacc: e.activation(
                    out=s_fm[:, oc * TM:(oc + 1) * TM], in_=p[:], func=AFT.Identity,
                    bias=bo_t[:, oc:oc + 1]), inc=(s_se, 1))
                w_se[oc] = s_se.n

            wload(0)
            wchain(0)
            for oc in range(16):
                if "w" in SAFE:
                    em.barrier(bar, self_sems)
                if oc + 1 < 16:
                    wload(oc + 1)
                wevac(oc)
                if oc + 1 < 16:
                    wchain(oc + 1)

        def phase_B(s, m, bbuf):
            (s_fm, fuse, gu8, uu8, d8s, gtmp, uptmp, gatmp, sgtmp, xh) = bbuf
            T0 = m * TM
            W("sync", s_te)
            src_ap = scr["xn_s"][s, :, :, T0:T0 + TM].rearrange("kc p t -> p kc t")
            O("sync", lambda e, a=src_ap: e.dma_start(
                out=xn_fm[:], in_=a), inc=(s_x, 16))
            W("tensor", s_x)
            gu_te = {}
            gu_se = {}

            def guload(og):
                buf_g, buf_u = gu8[og % 2], uu8[og % 2]
                if og - 2 >= 0:
                    W("scalar", s_te, gu_te[og - 2])
                gsrc = scr["g8"][og].rearrange("p k o -> p (k o)")
                usrc = scr["u8"][og].rearrange("p k o -> p (k o)")
                O("scalar", lambda e, a=gsrc, d=buf_g: e.dma_start(out=d[:, 0:4096], in_=a),
                  inc=(s_w, 16))
                O("scalar", lambda e, a=usrc, d=buf_u: e.dma_start(out=d[:, 0:4096], in_=a),
                  inc=(s_w, 16))

            def guchain(og):
                buf_g, buf_u = gu8[og % 2], uu8[og % 2]
                W("tensor", s_w)
                if og - 1 >= 0:
                    W("tensor", s_se, gu_se[og - 1])
                for kc in range(KC):
                    for i in range(2):
                        O("tensor", lambda e, kc=kc, i=i, b=buf_g: e.matmul(
                            pf[i][:], b[:, 0:4096].bitcast(FP8)[
                                :, kc * 256 + i * 128: kc * 256 + (i + 1) * 128],
                            xn_fm[:, kc * TM:(kc + 1) * TM],
                            start=(kc == 0), stop=(kc == KC - 1)))
                    for i in range(2):
                        O("tensor", lambda e, kc=kc, i=i, b=buf_u: e.matmul(
                            pf[2 + i][:], b[:, 0:4096].bitcast(FP8)[
                                :, kc * 256 + i * 128: kc * 256 + (i + 1) * 128],
                            xn_fm[:, kc * TM:(kc + 1) * TM],
                            start=(kc == 0), stop=(kc == KC - 1)),
                            inc=(s_te, 1) if (kc == KC - 1 and i == 1) else None)
                gu_te[og] = s_te.n

            def guevac(og):
                W("scalar", s_te, gu_te[og])
                for i in range(2):
                    oc = og * 2 + i
                    W("scalar", s_ve)
                    O("scalar", lambda e, i=i, oc=oc: e.activation(
                        out=gatmp[:], in_=pf[i][:], func=AFT.Silu,
                        scale=gs_t[:, oc:oc + 1]))
                    O("scalar", lambda e, i=i, oc=oc: e.activation(
                        out=uptmp[:], in_=pf[2 + i][:], func=AFT.Identity,
                        scale=us_t[:, oc:oc + 1]), inc=(s_se, 1))
                    W("vector", s_se)
                    O("vector", lambda e, oc=oc: e.tensor_tensor(
                        out=fuse[:, oc * TM:(oc + 1) * TM], in0=gatmp[:], in1=uptmp[:],
                        op=ALU.mult), inc=(s_ve, 1))
                gu_se[og] = s_se.n

            guload(0)
            guchain(0)
            for og in range(24):
                if "gu" in SAFE:
                    em.barrier(bar, self_sems)
                if og + 1 < 24:
                    guload(og + 1)
                guevac(og)
                if og + 1 < 24:
                    guchain(og + 1)

            W("tensor", s_ve)
            d_te = {}
            d_w = {}

            def dload(gi):
                ocg, hk = divmod(gi, 6)
                buf = d8s[gi % 2]
                if gi - 2 >= 0:
                    W("scalar", s_te, d_te[gi - 2])
                dsrc = scr["d8"][ocg, hk].rearrange("p k o -> p (k o)")
                O("scalar", lambda e, a=dsrc, d=buf: e.dma_start(out=d[:], in_=a),
                  inc=(s_w, 16))
                d_w[gi] = s_w.n

            def dchain(gi):
                ocg, hk = divmod(gi, 6)
                buf = d8s[gi % 2]
                W("tensor", s_w, d_w[gi])
                if hk == 0:
                    W("tensor", s_se)
                    if "down4" in SAFE:
                        W("tensor", s_ve)
                        W("tensor", s_st)
                for kc8 in range(8):
                    kc = hk * 8 + kc8
                    for i in range(2):
                        O("tensor", lambda e, kc=kc, kc8=kc8, i=i, b=buf: e.matmul(
                            pf[i][:], b[:].bitcast(FP8)[
                                :, kc8 * 256 + i * 128: kc8 * 256 + (i + 1) * 128],
                            fuse[:, kc * TM:(kc + 1) * TM],
                            start=(kc == 0), stop=(kc == OC_FF - 1)),
                            inc=(s_te, 1) if (kc8 == 7 and i == 1) else None)
                d_te[gi] = s_te.n

            dload(0)
            for ocg in range(8):
                if "down" in SAFE:
                    em.barrier(bar, self_sems)
                if "down5" in SAFE:
                    W("sync", s_se)
                    W("sync", s_te)
                W("sync", s_st)
                W("sync", s_ve)
                for tb in range(4):
                    O("sync", lambda e, s=s, T0=T0, tb=tb, ocg=ocg: e.dma_start(
                        out=xh[tb][:, 0:256],
                        in_=din["x"][s, T0 + tb * 128:T0 + (tb + 1) * 128,
                                     ocg * 256:(ocg + 1) * 256]), inc=(s_x, 16))
                for hk in range(6):
                    gi = ocg * 6 + hk
                    if gi + 1 < 48:
                        dload(gi + 1)
                    dchain(gi)
                if "down1" in SAFE:
                    em.barrier(bar, self_sems)
                W("scalar", s_te)
                W("vector", s_x)
                for i in range(2):
                    oc = ocg * 2 + i
                    W("scalar", s_ve)
                    O("scalar", lambda e, i=i, oc=oc: e.activation(
                        out=gtmp[:], in_=pf[i][:], func=AFT.Silu,
                        scale=ds_t[:, oc:oc + 1]), inc=(s_se, 1))
                    W("vector", s_se)
                    O("vector", lambda e, oc=oc: e.tensor_tensor(
                        out=sgtmp[:], in0=gtmp[:], in1=s_fm[:, oc * TM:(oc + 1) * TM],
                        op=ALU.mult), inc=(s_ve, 1))
                    W("tensor", s_ve)
                    for tb in range(4):
                        O("tensor", lambda e, tb=tb, i=i: e.transpose(
                            pf[2 + tb][:, i * 128:(i + 1) * 128],
                            sgtmp[:, tb * 128:(tb + 1) * 128], identf_t[:]),
                            inc=(s_te, 1) if tb == 3 else None)
                    W("vector", s_te)
                if "down2" in SAFE:
                    em.barrier(bar, self_sems)
                for tb in range(4):
                    O("vector", lambda e, tb=tb: e.scalar_tensor_tensor(
                        out=xh[tb][:, 0:256], in0=pf[2 + tb][:, 0:256], scalar=1.0,
                        in1=xh[tb][:, 0:256], op0=ALU.mult, op1=ALU.add),
                        inc=(s_ve, 1))
                if "down3" in SAFE:
                    em.barrier(bar, self_sems)
                W("sync", s_ve)
                for tb in range(4):
                    O("sync", lambda e, s=s, T0=T0, tb=tb, ocg=ocg: e.dma_start(
                        out=y_out[s, T0 + tb * 128:T0 + (tb + 1) * 128,
                                  ocg * 256:(ocg + 1) * 256],
                        in_=xh[tb][:, 0:256]), inc=(s_st, 16))

        for s in range(nseq):
            ces = ExitStack()
            with ces:
                qp_sin = ces.enter_context(nc.sbuf_tensor(f"qp_sin{s}", [64, 16 * TM], BF16))
                qp_cos = ces.enter_context(nc.sbuf_tensor(f"qp_cos{s}", [64, 16 * TM], BF16))
                kp_sin = ces.enter_context(nc.sbuf_tensor(f"kp_sin{s}", [64, 8 * TM], BF16))
                kp_cos = ces.enter_context(nc.sbuf_tensor(f"kp_cos{s}", [64, 8 * TM], BF16))
                v_all = ces.enter_context(nc.sbuf_tensor(f"v_all{s}", [128, 8 * TM], BF16))
                vt_sb = ces.enter_context(nc.sbuf_tensor(f"vt_sb{s}", [128, 128], BF16))
                vt_sb2 = ces.enter_context(nc.sbuf_tensor(f"vt_sb2{s}", [128, 128], BF16))
                sm_sb = ces.enter_context(nc.sbuf_tensor(f"sm_sb{s}", [128, 256], BF16))
                sm_sb2 = ces.enter_context(nc.sbuf_tensor(f"sm_sb2{s}", [128, 256], BF16))
                xt = ces.enter_context(nc.sbuf_tensor(f"xt{s}", [128, HID], F32))
                scr2k = ces.enter_context(nc.sbuf_tensor(f"scr2k{s}", [128, HID], BF16))
                xn_tm = ces.enter_context(nc.sbuf_tensor(f"xn_tm{s}", [128, HID], BF16))
                lnst = ces.enter_context(nc.sbuf_tensor(f"lnst{s}", [128, 8], F32))
                qbf = ces.enter_context(nc.sbuf_tensor(f"qbf{s}", [128, TM], BF16))
                qrot = ces.enter_context(nc.sbuf_tensor(f"qrot{s}", [128, TM], BF16))
                t1f = ces.enter_context(nc.sbuf_tensor(f"t1f{s}", [128, TM], F32))
                sqf = ces.enter_context(nc.sbuf_tensor(f"sqf{s}", [128, TM], F32))
                t2f = sqf
                nbf = ces.enter_context(nc.sbuf_tensor(f"nbf{s}", [64, 1024], F32))
                rbf = ces.enter_context(nc.sbuf_tensor(f"rbf{s}", [64, 1024], F32))
                p_sb = ces.enter_context(nc.sbuf_tensor(f"p_sb{s}", [64, 1024], F32))
                pred = ces.enter_context(nc.sbuf_tensor(f"pred{s}", [64, 1024], F32))
                ki_sb = ces.enter_context(nc.sbuf_tensor(f"ki_sb{s}", [64, 1024], I32))
                ln = tuple(lnst[:, i:i + 1] for i in range(7))
                cbuf = (qp_sin, qp_cos, kp_sin, kp_cos, v_all, vt_sb, vt_sb2, sm_sb, sm_sb2, xt,
                        scr2k, xn_tm, ln, qbf, qrot, t1f, t2f, sqf, nbf, rbf,
                        p_sb, pred)
                phase_C(s, cbuf)
                em.barrier(bar, self_sems)
                if dbg and s == 0:
                    O("sync", lambda e: e.dma_start(out=dbg_t["dbg_am"], in_=am_sb[:]),
                      inc=(s_st, 16))
                    O("sync", lambda e: e.dma_start(out=dbg_t["dbg_qps"], in_=qp_sin[:]),
                      inc=(s_st, 16))
                    O("sync", lambda e: e.dma_start(out=dbg_t["dbg_kps"], in_=kp_sin[:]),
                      inc=(s_st, 16))
                    O("sync", lambda e: e.dma_start(out=dbg_t["dbg_vall"], in_=v_all[:]),
                      inc=(s_st, 16))
                    O("sync", lambda e: e.dma_start(out=dbg_t["dbg_xn"],
                                                    in_=scr["xn_s"][0]), inc=(s_st, 16))
                    O("sync", lambda e: e.dma_start(out=dbg_t["dbg_wqT"],
                                                    in_=scr["wqT"][0]), inc=(s_st, 16))
                    O("sync", lambda e: e.dma_start(out=dbg_t["dbg_pred"], in_=pred[:]),
                      inc=(s_st, 16))
                    O("sync", lambda e: e.dma_start(out=dbg_t["dbg_psb"], in_=p_sb[:]),
                      inc=(s_st, 16))
                    O("sync", lambda e: e.dma_start(out=dbg_t["dbg_rbf"], in_=rbf[:]),
                      inc=(s_st, 16))
                    O("sync", lambda e: e.dma_start(out=dbg_t["dbg_qrot"], in_=qrot[:]),
                      inc=(s_st, 16))
                    O("sync", lambda e: e.dma_start(out=dbg_t["dbg_sqf"], in_=sqf[:]),
                      inc=(s_st, 16))
                    O("sync", lambda e: e.dma_start(out=dbg_t["dbg_nbf"], in_=nbf[:]),
                      inc=(s_st, 16))
                    W("sync", s_st)
            bes = ExitStack()
            with bes:
                s_fm = bes.enter_context(nc.sbuf_tensor(f"s_fm{s}", [128, 16 * TM], BF16))
                fuse = bes.enter_context(nc.sbuf_tensor(f"fuse{s}", [128, OC_FF * TM], BF16))
                gu8a = bes.enter_context(nc.sbuf_tensor(f"gu8a{s}", [128, 4096], U8))
                gu8b = bes.enter_context(nc.sbuf_tensor(f"gu8b{s}", [128, 4096], U8))
                uu8a = bes.enter_context(nc.sbuf_tensor(f"uu8a{s}", [128, 4096], U8))
                uu8b = bes.enter_context(nc.sbuf_tensor(f"uu8b{s}", [128, 4096], U8))
                d8sa = bes.enter_context(nc.sbuf_tensor(f"d8sa{s}", [128, 2048], U8))
                d8sb = bes.enter_context(nc.sbuf_tensor(f"d8sb{s}", [128, 2048], U8))
                gtmp = bes.enter_context(nc.sbuf_tensor(f"gtmp{s}", [128, TM], F32))
                uptmp = bes.enter_context(nc.sbuf_tensor(f"uptmp{s}", [128, TM], BF16))
                gatmp = bes.enter_context(nc.sbuf_tensor(f"gatmp{s}", [128, TM], BF16))
                sgtmp = bes.enter_context(nc.sbuf_tensor(f"sgtmp{s}", [128, TM], F32))
                xh0 = bes.enter_context(nc.sbuf_tensor(f"xh0{s}", [128, 256], F32))
                xh1 = bes.enter_context(nc.sbuf_tensor(f"xh1{s}", [128, 256], F32))
                xh2 = bes.enter_context(nc.sbuf_tensor(f"xh2{s}", [128, 256], F32))
                xh3 = bes.enter_context(nc.sbuf_tensor(f"xh3{s}", [128, 256], F32))
                bbuf = (s_fm, fuse, (gu8a, gu8b), (uu8a, uu8b), (d8sa, d8sb),
                        gtmp, uptmp, gatmp, sgtmp, (xh0, xh1, xh2, xh3))
                for m in range(nmacro):
                    phase_W(s, m, s_fm)
                    em.barrier(bar, self_sems)
                    if dbg and s == 0 and m == 0:
                        O("sync", lambda e: e.dma_start(out=dbg_t["dbg_sfm"], in_=s_fm[:]),
                          inc=(s_st, 16))
                        W("sync", s_st)
                    if dbg and s == 0 and m == 3:
                        O("sync", lambda e: e.dma_start(out=dbg_t["dbg_sfm3"], in_=s_fm[:]),
                          inc=(s_st, 16))
                        W("sync", s_st)
                    phase_B(s, m, bbuf)
                    em.barrier(bar, self_sems)
                    if dbg and s == 0 and m == 0:
                        O("sync", lambda e: e.dma_start(out=dbg_t["dbg_fuse"], in_=fuse[:]),
                          inc=(s_st, 16))
                        W("sync", s_st)
                    if dbg and s == 0 and m == 3:
                        O("sync", lambda e: e.dma_start(out=dbg_t["dbg_fuse3"], in_=fuse[:]),
                          inc=(s_st, 16))
                        W("sync", s_st)

        W("sync", s_st)

        with nc.Block() as block:
            @block.sync
            def _(e):
                for f in em.prog["sync"]:
                    f(e)

            @block.vector
            def _(e):
                for f in em.prog["vector"]:
                    f(e)

            @block.scalar
            def _(e):
                for f in em.prog["scalar"]:
                    f(e)

            @block.tensor
            def _(e):
                for f in em.prog["tensor"]:
                    f(e)

    return nc


_NC_CACHE = {}


def kernel(**inputs):
    consts = _consts()
    if "nc" not in _NC_CACHE:
        _NC_CACHE["nc"] = build_nc()
    nc = _NC_CACHE["nc"]

    x = np.ascontiguousarray(inputs["L0_attn_hidden_states"])
    in_maps = []
    for c in range(NCORES):
        m = {
            "x": x[c * NSEQ:(c + 1) * NSEQ],
            "wq": inputs["L0_attn_Wq"], "bq": inputs["L0_attn_bq"],
            "wk": inputs["L0_attn_Wk"], "bk": inputs["L0_attn_bk"],
            "wv": inputs["L0_attn_Wv"], "bv": inputs["L0_attn_bv"],
            "wo": inputs["L0_attn_Wo"], "bo": inputs["L0_attn_bo"],
            "rmat": inputs["L0_attn_rand_matrix"],
            "gw": inputs["L0_ffn_gate_w"], "uw": inputs["L0_ffn_up_w"],
            "dw": inputs["L0_ffn_down_w"],
            "gs": inputs["L0_ffn_gate_scale"], "us": inputs["L0_ffn_up_scale"],
            "ds": inputs["L0_ffn_down_scale"],
        }
        m.update(consts)
        in_maps.append(m)

    res = run_bass_kernel_spmd(nc, in_maps, core_ids=list(range(NCORES)))
    out = np.concatenate([res.results[c]["y"] for c in range(NCORES)], axis=0)
    return np.ascontiguousarray(out.astype(np.float32))
